# version 25
# baseline (speedup 1.0000x reference)
import numpy as np
import ml_dtypes
from contextlib import ExitStack

import concourse.bass as bass
import concourse.mybir as mybir
from concourse.bass_utils import run_bass_kernel_spmd

F32 = mybir.dt.float32
BF16 = mybir.dt.bfloat16
I32 = mybir.dt.int32
AF = mybir.ActivationFunctionType
AX = mybir.AxisListType
OP = mybir.AluOpType

T = 2048
D = 1024
NCH = 8
NTT = 16
NTG = 4
KEFF = 1152.0
NHEAVY = 1024.0


class Ctr:
    def __init__(self, nc, es, name):
        self.h = es.enter_context(nc.semaphore(name))
        self.n = 0


def build_program(dbg=(), phase_limit=99):
    nc = bass.Bass("TRN2", target_bir_lowering=False, debug=False)
    dbg = set(dbg)

    dtn = lambda n, s, d, k: nc.dram_tensor(n, s, d, kind=k).ap()
    xT = dtn("xT", [D, T], F32, "ExternalInput")
    W1 = dtn("W1", [D, 4 * D], F32, "ExternalInput")
    W2d = dtn("W2", [D, D], F32, "ExternalInput")
    w1lT = dtn("w1lT", [D, 512], F32, "ExternalInput")
    w2lT = dtn("w2lT", [512, D], F32, "ExternalInput")
    w1hT = dtn("w1hT", [D, 4096], F32, "ExternalInput")
    w2hT = dtn("w2hT", [4096, D], F32, "ExternalInput")
    WT = {"q": dtn("WqT", [D, D], F32, "ExternalInput"),
          "k": dtn("WkT", [D, D], F32, "ExternalInput"),
          "v": dtn("WvT", [D, D], F32, "ExternalInput")}
    WO = {"q": dtn("Wqo", [D, D], F32, "ExternalInput"),
          "k": dtn("Wko", [D, D], F32, "ExternalInput"),
          "v": dtn("Wvo", [D, D], F32, "ExternalInput")}
    w1wt = dtn("w1wt", [32, 128, 1024], F32, "ExternalInput")
    w2wt = dtn("w2wt", [32, 128, 1024], F32, "ExternalInput")
    h1wt = dtn("h1wt", [32, 128, 4096], F32, "ExternalInput")
    h2wt = dtn("h2wt", [32, 128, 4096], F32, "ExternalInput")
    w3wt = dtn("w3wt", [8, 128, 4096], F32, "ExternalInput")
    w1ht = dtn("w1ht", [32, 128, 1024], F32, "ExternalInput")
    W1p = dtn("W1p", [8, 128, 8, 512], F32, "ExternalInput")
    w1lt = dtn("w1lt", [4, 128, 1024], F32, "ExternalInput")
    w2lt = dtn("w2lt", [8, 128, 512], F32, "ExternalInput")
    VCOL = dtn("VCOL", [128, 288], F32, "ExternalInput")
    VROW = dtn("VROW", [1, 8192], F32, "ExternalInput")
    CONST = dtn("CONST", [128, 260], F32, "ExternalInput")
    yTo = dtn("yT", [D, T], F32, "ExternalOutput")
    xsp = dtn("xsp", [D, T], F32, "Internal")

    es = ExitStack()

    class PView:
        def __init__(self, t, i):
            self.t, self.i = t, i
        def __getitem__(self, idx):
            if isinstance(idx, tuple):
                rs, cs = idx
            else:
                rs, cs = idx, slice(None)
            return self.t[rs, self.i, cs]

    _cms = {}
    def sbt(n, s, d):
        cm = nc.sbuf_tensor(n, s, d)
        t = cm.__enter__()
        _cms[n] = cm
        return t
    def sfree(*names):
        for n in names:
            _cms.pop(n).__exit__(None, None, None)
    psALL = es.enter_context(nc.psum_tensor("psA", [128, 4, 512], F32))
    psB = [PView(psALL, i) for i in range(4)] + \
          [es.enter_context(nc.psum_tensor(f"pb{i}", [128, 512], F32)) for i in range(4, 8)]

    vcol = sbt("vcol", [128, 288], F32)
    cst = sbt("cst", [128, 260], F32)
    tri_bf = sbt("tri_bf", [128, 128], BF16)
    ones_bf = sbt("ones_bf", [128, 128], BF16)
    onesr_bf = sbt("onesr_bf", [1, 128], BF16)
    onesr_f = sbt("onesr_f", [1, 128], F32)
    ones2h_bf = sbt("ones2h_bf", [128, 2], BF16)
    onesc_bf = sbt("onesc_bf", [128, 1], BF16)
    col = {nm: sbt("col_" + nm, [128, 16], F32) for nm in
           ("s", "E", "R", "junk", "part", "sat", "pref", "sel",
            "mua", "iva", "tmpa", "tmpb")}
    satc_bf = sbt("satc_bf", [128, 16], BF16)
    tiny = sbt("tiny", [1, 64], F32)
    xn = sbt("xn", [128, NCH, T], BF16)
    smix = sbt("smix", [128, NCH, T], BF16)
    slab = sbt("slab", [1, T], F32)
    slab2 = sbt("slab2", [1, T], F32)
    bcA = sbt("bcA", [128, T], F32)
    bcB = sbt("bcB", [128, T], F32)
    _ci = {k: i for i, k in enumerate(
        ("rcell", "rsum", "lo", "hi", "mid", "cnt", "npred", "w0", "w1", "w2"))}
    cell = lambda nm: tiny[0:1, _ci[nm]:_ci[nm] + 1]

    eye = cst[:, 0:128]
    tri_f = cst[:, 128:256]
    ones2h_f = cst[:, 256:258]
    onesc_f = cst[:, 258:259]

    qst = Ctr(nc, es, "qst")
    _streams = {}
    def stream(name):
        if name not in _streams:
            _streams[name] = Ctr(nc, es, name)
        return _streams[name]
    ct = Ctr(nc, es, "ct")
    cs = Ctr(nc, es, "cs")
    cv = Ctr(nc, es, "cv")

    Q = {"t": [], "s": [], "v": [], "g": [], "y": []}

    def y_dma(st, out, in_):
        st.n += 16
        Q["y"].append(lambda e, o=out, i=in_, h=st.h: e.dma_start(out=o, in_=i).then_inc(h, 16))
        return st.n

    def g_dma(st, out, in_):
        st.n += 16
        Q["g"].append(lambda e, o=out, i=in_, h=st.h: e.dma_start(out=o, in_=i).then_inc(h, 16))
        return st.n

    def y_store(out, in_):
        qst.n += 16
        Q["y"].append(lambda e, o=out, i=in_: e.dma_start(out=o, in_=i).then_inc(qst.h, 16))
        return qst.n

    def wt(q, ctr, v):
        if v > 0:
            Q[q].append(lambda e, h=ctr.h, vv=v: e.wait_ge(h, vv))

    def TN(emit): Q["t"].append(emit)
    def SN(emit): Q["s"].append(emit)
    def VN(emit): Q["v"].append(emit)

    def TI(emit):
        ct.n += 1
        Q["t"].append(lambda e, f=emit: f(e).then_inc(ct.h, 1))
        return ct.n

    def SI(emit):
        cs.n += 1
        Q["s"].append(lambda e, f=emit: f(e).then_inc(cs.h, 1))
        return cs.n

    def VI(emit):
        cv.n += 1
        Q["v"].append(lambda e, f=emit: f(e).then_inc(cv.h, 1))
        return cv.n

    VD = lambda: VN(lambda v: v.drain())

    dbg_list = []
    def dbg_add(name, src, shape, dtype=F32):
        if name in dbg:
            d = nc.dram_tensor("dbg_" + name, list(shape), dtype, kind="ExternalOutput").ap()
            wt("y", ct, ct.n)
            wt("y", cs, cs.n)
            wt("y", cv, cv.n)
            st = stream("qdbg_" + name)
            st.n += 16
            Q["y"].append(lambda e, o=d, i=src, h=st.h: e.dma_start(out=o, in_=i).then_inc(h, 16))
            dbg_list.append(st)
            for qq in ("t", "s", "v", "g"):
                wt(qq, st, 16)

    VC = {}
    _o = 0
    for nm, k in (("b1", 32), ("b2", 8), ("gm", 8), ("bt", 8), ("rt", 8),
                  ("gl", 8), ("gh", 8), ("b1l", 4), ("b2l", 8), ("b1h", 32),
                  ("bq", 8), ("bk", 8), ("bv", 8), ("w1s", 32), ("w2s", 32),
                  ("h1s", 32), ("h2s", 32), ("w3s", 8)):
        VC[nm] = _o; _o += k
    vcs = lambda nm, j: vcol[:, VC[nm] + j:VC[nm] + j + 1]
    VR = {}
    _o = 0
    for nm, k in (("b1", 4096), ("b2h", 1024), ("bv", 1024), ("b2", 1024)):
        VR[nm] = _o; _o += k
    vrs = lambda nm, a, b: VROW[0:1, VR[nm] + a:VR[nm] + b]

    qc = stream("qc")
    y_dma(qc, vcol[:], VCOL)
    ld_cst = y_dma(qc, cst[:], CONST)
    wt("v", qc, ld_cst)
    VI(lambda v: v.tensor_copy(tri_bf[:], tri_f))
    VI(lambda v: v.memset(ones_bf[:], 1.0))
    VI(lambda v: v.memset(onesr_bf[:], 1.0))
    VI(lambda v: v.memset(onesr_f[:], 1.0))
    VI(lambda v: v.memset(onesc_bf[:], 1.0))
    cv_cst = VI(lambda v: v.tensor_copy(ones2h_bf[:], ones2h_f))
    wt("t", cv, cv_cst)
    wt("s", cv, cv_cst)
    wt("g", qc, ld_cst)

    sqrot = [sbt(f"sqrot{i}", [128, T], F32) for i in range(2)]
    sel_bf = sbt("sel_bf", [1, T], BF16)
    selb = sbt("selb", [128, T], BF16)
    b2h_bf = sbt("b2h_bf", [1, D], BF16)
    xrot = [sbt(f"xrot{i}", [128, T], F32) for i in range(4)]
    xfrot = sqrot

    def emit_ln(x_src, with_s, stats_ready=False):
        xv = x_src.rearrange("(c p) t -> p c t", p=128)
        tdone = {}
        qx = [stream(f"qx{i}") for i in range(4)]
        for c in range(0 if stats_ready else NCH):
            sl = c % 4
            sq_sl = c % 2
            if c >= 4:
                wt("y", ct, tdone[c - 4])
            ldx = y_dma(qx[sl], xrot[sl][:], xv[:, c, :])
            if c >= 2:
                wt("v", ct, tdone[c - 2])
            wt("v", qx[sl], ldx)
            cvsq = VI(lambda v, a=sqrot[sq_sl], b=xrot[sl]: v.tensor_tensor(
                a[:], b[:], b[:], OP.mult))
            wt("t", qx[sl], ldx)
            wt("t", cv, cvsq)
            for g in range(NTG):
                TN(lambda t, b=xrot[sl], g=g, c=c: t.matmul(
                    psB[g][0:1, 0:512], onesc_f, b[:, g * 512:(g + 1) * 512],
                    start=(c == 0), stop=(c == NCH - 1)))
                emitter = TN if g < NTG - 1 else TI
                emitter(lambda t, a=sqrot[sq_sl], g=g, c=c: t.matmul(
                    psB[4 + g][0:1, 0:512], onesc_f, a[:, g * 512:(g + 1) * 512],
                    start=(c == 0), stop=(c == NCH - 1)))
            tdone[c] = ct.n
        if not stats_ready:
            wt("s", ct, tdone[NCH - 1])
            for g in range(NTG):
                SN(lambda s, g=g: s.copy(slab[0:1, g * 512:(g + 1) * 512], psB[g][0:1, 0:512]))
                emitter = SN if g < NTG - 1 else SI
                emitter(lambda s, g=g: s.copy(slab2[0:1, g * 512:(g + 1) * 512],
                                              psB[4 + g][0:1, 0:512]))
        sev = cs.n
        wt("v", cs, sev)
        VN(lambda v: v.tensor_scalar(slab[:], slab[:], 1.0 / D, None, OP.mult))
        VD()
        VN(lambda v: v.tensor_scalar(slab2[:], slab2[:], 1.0 / D, None, OP.mult))
        VN(lambda v: v.tensor_tensor(sqrot[0][0:1, 0:T], slab[:], slab[:], OP.mult))
        VD()
        VN(lambda v: v.tensor_tensor(slab2[:], slab2[:], sqrot[0][0:1, 0:T], OP.subtract))
        VD()
        cvv = VI(lambda v: v.tensor_scalar(slab2[:], slab2[:], 1e-5, None, OP.add))
        wt("s", cv, cvv)
        css = SI(lambda s: s.activation(slab2[:], slab2[:], AF.Sqrt))
        wt("v", cs, css)
        cvi = VI(lambda v: v.reciprocal(slab2[:], slab2[:]))
        wt("t", cv, cvi)
        for g in range(NTG):
            tb = TI(lambda t, g=g: t.matmul(psB[0][:, :], onesr_f[:],
                                            slab[0:1, g * 512:(g + 1) * 512], start=True, stop=True))
            wt("s", ct, tb)
            sb1 = SI(lambda s, g=g: s.copy(bcA[:, g * 512:(g + 1) * 512], psB[0][:, :]))
            wt("t", cs, sb1)
            tb2 = TI(lambda t, g=g: t.matmul(psB[1][:, :], onesr_f[:],
                                             slab2[0:1, g * 512:(g + 1) * 512], start=True, stop=True))
            wt("s", ct, tb2)
            sb2 = SI(lambda s, g=g: s.copy(bcB[:, g * 512:(g + 1) * 512], psB[1][:, :]))
            wt("t", cs, sb2)
        wt("v", cs, cs.n)
        tdone2 = {}
        for c in range(NCH):
            sl = c % 4
            sq_sl = c % 2
            if c >= 4:
                _, gctr, gval = tdone2[c - 4]
                wt("y", gctr, gval)
            elif not stats_ready:
                wt("y", ct, tdone[NCH - 4 + c])
            ldx = y_dma(qx[sl], xrot[sl][:], xv[:, c, :])
            wt("v", qx[sl], ldx)
            if c >= 2:
                _, gctr, gval = tdone2[c - 2]
                wt("v", gctr, gval)
            VN(lambda v, a=xfrot[sq_sl], b=xrot[sl]: v.tensor_tensor(a[:], b[:], bcA[:], OP.subtract))
            VD()
            if with_s:
                VN(lambda v, a=xfrot[sq_sl]: v.tensor_tensor(a[:], a[:], bcB[:], OP.mult))
                VD()
                cvx = VI(lambda v, a=xfrot[sq_sl], c=c: v.tensor_copy(xn[:, c, :], a[:]))
            else:
                cvx = VI(lambda v, a=xfrot[sq_sl], c=c: v.tensor_tensor(
                    xn[:, c, :], a[:], bcB[:], OP.mult))
            if with_s:
                wt("t", cv, cvx)
                for tt in range(NTT):
                    emitter = TN if tt < NTT - 1 else TI
                    emitter(lambda t, a=xfrot[sq_sl], c=c, tt=tt: t.matmul(
                        psB[6][:, tt:tt + 1], a[:, tt * 128:(tt + 1) * 128],
                        vcs("rt", c), start=(c == 0 and tt == 0),
                        stop=(c == NCH - 1 and tt == NTT - 1)))
                tdone2[c] = ("t", ct, ct.n)
            else:
                tdone2[c] = ("v", cv, cvx)
        if with_s:
            wt("s", ct, tdone2[NCH - 1][2])
            return SI(lambda s: s.copy(col["s"][:], psB[6][:, 0:16]))
        return None

    cs_scol = emit_ln(xT, with_s=True)
    dbg_add("xn0", xn[:, :, :], (128, NCH * T), BF16)
    dbg_add("s_col", col["s"][:], (128, 16))

    def routing_gen():
        wt("v", cs, cs_scol)
        cve = VI(lambda v: v.tensor_scalar(col["E"][:], col["s"][:], 44.0, None, OP.min))
        wt("s", cv, cve)
        csE = SI(lambda s: s.activation(col["E"][:], col["E"][:], AF.Exp))
        wt("v", cs, csE)
        VI(lambda v: v.memset(col["R"][:, 0:1], float(np.float32(2048.0 / KEFF))))
        VN(lambda v: v.tensor_scalar(col["junk"][:], col["s"][:], -1.0, None, OP.mult))
        VD()
        VN(lambda v: v.reduce_max(col["part"][:, 0:1], col["junk"][:], axis=AX.X))
        cvt0 = VI(lambda v: v.reduce_max(col["tmpa"][:, 0:1], col["s"][:], axis=AX.X))
        wt("t", cv, cvt0)
        tr1 = TI(lambda t: t.transpose(psB[4][0:1, 0:128], col["part"][:, 0:1], eye))
        tr2 = TI(lambda t: t.transpose(psB[5][0:1, 0:128], col["tmpa"][:, 0:1], eye))
        wt("v", ct, tr2)
        VN(lambda v: v.reduce_max(cell("lo"), psB[4][0:1, 0:128], axis=AX.X))
        VD()
        VN(lambda v: v.tensor_scalar(cell("lo"), cell("lo"), -1.0, None, OP.mult))
        VN(lambda v: v.reduce_max(cell("hi"), psB[5][0:1, 0:128], axis=AX.X))
        VD()
        cvhi = VI(lambda v: v.tensor_scalar(cell("hi"), cell("hi"), 1.0, None, OP.add))
        yield
        prevRc = 0
        for it in range(49):
            wt("v", cv, prevRc)
            VN(lambda v: v.tensor_scalar(col["junk"][:], col["E"][:], col["R"][:, 0:1],
                                         0.0, OP.min, OP.add))
            VD()
            cva = VI(lambda v: v.reduce_sum(col["part"][:, 1:2], col["junk"][:], axis=AX.X))
            wt("t", cv, cva)
            ta = TI(lambda t: t.matmul(psB[4][0:1, 0:1], col["part"][:, 1:2], onesc_f,
                                       start=True, stop=True))
            wt("v", ct, ta)
            cvr2 = VI(lambda v: v.tensor_scalar(cell("rcell"), psB[4][0:1, 0:1], 1.0 / KEFF,
                                                None, OP.mult))
            VD()
            wt("t", cv, cvr2)
            tb = TI(lambda t: t.matmul(psB[4][:, 1:2], onesr_f[:], cell("rcell"),
                                       start=True, stop=True))
            wt("v", ct, tb)
            prevRc = VI(lambda v: v.tensor_copy(col["R"][:, 0:1], psB[4][:, 1:2]))
            VD()
            if it % 2 == 1:
                yield
        dbg_add("Rfin", col["R"][:, 0:1], (128, 1))
        wt("v", cv, cvhi)
        for it in range(25):
            VN(lambda v: v.tensor_tensor(cell("mid"), cell("lo"), cell("hi"), OP.add))
            VD()
            cvm = VI(lambda v: v.tensor_scalar(cell("mid"), cell("mid"), 0.5, None, OP.mult))
            wt("t", cv, cvm)
            tm = TI(lambda t: t.matmul(psB[5][:, 1:2], onesr_f[:], cell("mid"),
                                       start=True, stop=True))
            wt("v", ct, tm)
            cvmc = VI(lambda v: v.tensor_copy(col["tmpa"][:, 2:3], psB[5][:, 1:2]))
            VD()
            wt("v", cv, cvmc)
            VN(lambda v: v.tensor_scalar(col["junk"][:], col["s"][:], col["tmpa"][:, 2:3],
                                         0.0, OP.is_ge, OP.add))
            VD()
            cvc = VI(lambda v: v.reduce_sum(col["part"][:, 3:4], col["junk"][:], axis=AX.X))
            wt("t", cv, cvc)
            tc = TI(lambda t: t.matmul(psB[5][0:1, 0:1], col["part"][:, 3:4], onesc_f,
                                       start=True, stop=True))
            wt("v", ct, tc)
            VN(lambda v: v.tensor_scalar(cell("cnt"), psB[5][0:1, 0:1], NHEAVY, None, OP.is_ge))
            VD()
            VN(lambda v: v.tensor_scalar(cell("npred"), cell("cnt"), -1.0, 1.0, OP.mult, OP.add))
            VN(lambda v: v.tensor_tensor(cell("w0"), cell("mid"), cell("lo"), OP.subtract))
            VN(lambda v: v.tensor_tensor(cell("w1"), cell("mid"), cell("hi"), OP.subtract))
            VD()
            VN(lambda v: v.tensor_tensor(cell("w0"), cell("w0"), cell("cnt"), OP.mult))
            VN(lambda v: v.tensor_tensor(cell("w1"), cell("w1"), cell("npred"), OP.mult))
            VD()
            VN(lambda v: v.tensor_tensor(cell("lo"), cell("lo"), cell("w0"), OP.add))
            cv_it = VI(lambda v: v.tensor_tensor(cell("hi"), cell("hi"), cell("w1"), OP.add))
            VD()
            if it % 2 == 1:
                yield
        cv_tau = cv_it
        wt("t", cv, cv_tau)
        ttb = TI(lambda t: t.matmul(psB[5][:, 1:2], onesr_f[:], cell("lo"), start=True, stop=True))
        wt("v", ct, ttb)
        cvtb2 = VI(lambda v: v.tensor_copy(col["tmpb"][:, 0:1], psB[5][:, 1:2]))
        VD()
        wt("v", cv, max(cvtb2, prevRc))
        VN(lambda v: v.tensor_scalar(col["sat"][:], col["E"][:], col["R"][:, 0:1],
                                     0.0, OP.is_ge, OP.add))
        VD()
        cvsat = VI(lambda v: v.tensor_copy(satc_bf[:], col["sat"][:]))
        yield
        wt("t", cv, cvsat)
        for tt in range(NTT):
            for c in range(tt + 1):
                emitter = TI if (tt == NTT - 1 and c == tt) else TN
                emitter(lambda t, tt=tt, c=c, m=(tri_bf if c == tt else ones_bf): t.matmul(
                    psB[5][:, 32 + tt:33 + tt], m[:], satc_bf[:, c:c + 1],
                    start=(tt == 0 and c == 0), stop=(tt == NTT - 1 and c == tt)))
        tpref = ct.n
        wt("s", ct, tpref)
        cspref = SI(lambda s: s.copy(col["pref"][:], psB[5][:, 32:48]))
        wt("v", cs, cspref)
        VN(lambda v: v.tensor_scalar(col["tmpa"][:], col["pref"][:], NHEAVY, None, OP.is_le))
        VN(lambda v: v.tensor_scalar(col["junk"][:], col["s"][:], col["tmpb"][:, 0:1],
                                     0.0, OP.is_ge, OP.add))
        VD()
        VN(lambda v: v.tensor_tensor(col["tmpa"][:], col["tmpa"][:], col["junk"][:], OP.subtract))
        VD()
        VN(lambda v: v.tensor_tensor(col["tmpa"][:], col["tmpa"][:], col["sat"][:], OP.mult))
        VD()
        cvsel = VI(lambda v: v.tensor_tensor(col["sel"][:], col["junk"][:], col["tmpa"][:], OP.add))
        dbg_add("sel_col", col["sel"][:], (128, 16))
        yield
        wt("t", cv, cvsel)
        for tt in range(NTT):
            trr = TI(lambda t, tt=tt: t.transpose(psB[4][0:1, 0:128], col["sel"][:, tt:tt + 1], eye))
            wt("s", ct, trr)
            srr = SI(lambda s, tt=tt: s.copy(sel_bf[0:1, tt * 128:(tt + 1) * 128],
                                             psB[4][0:1, 0:128]))
            wt("t", cs, srr)
            if tt % 6 == 5:
                yield
        for g in range(NTG):
            tsb = TI(lambda t, g=g: t.matmul(psB[4][:, :], onesr_bf[:],
                                             sel_bf[0:1, g * 512:(g + 1) * 512], start=True, stop=True))
            wt("s", ct, tsb)
            ssb = SI(lambda s, g=g: s.copy(selb[:, g * 512:(g + 1) * 512], psB[4][:, :]))
            wt("t", cs, ssb)
        dbg_add("sel_row", sel_bf[:], (1, T), BF16)

    rg = routing_gen()
    if phase_limit >= 2:
        next(rg, None)

    rms_t = {}
    proj_ev = {}
    v_ev = {}
    sc_ev = {}
    av_mm = {}
    au_ev = {}
    ghmask = {}
    ghmm = {}
    sg_use = {}
    spill_n = {}
    xng_v = {}
    gl_use = {}
    if phase_limit >= 3:
        xn_ready = cv.n
        for c in range(NCH):
            sl = c % 2
            if c >= 2:
                wt("v", ct, rms_t[c - 2])
            cvq = VI(lambda v, sl=sl, c=c: v.tensor_tensor(
                sqrot[sl][:], xn[:, c, :], xn[:, c, :], OP.mult))
            wt("t", cv, cvq)
            for g in range(NTG):
                emitter = TN if g < NTG - 1 else TI
                emitter(lambda t, sl=sl, g=g, c=c: t.matmul(
                    psB[g][0:1, 0:512], onesc_f, sqrot[sl][:, g * 512:(g + 1) * 512],
                    start=(c == 0), stop=(c == NCH - 1)))
            rms_t[c] = ct.n
        wt("s", ct, rms_t[NCH - 1])
        for g in range(NTG):
            emitter = SN if g < NTG - 1 else SI
            emitter(lambda s, g=g: s.copy(slab[0:1, g * 512:(g + 1) * 512], psB[g][0:1, 0:512]))
        s_rss = cs.n
        wt("s", cs, s_rss)
        css2 = SI(lambda s: s.activation(slab[:], slab[:], AF.Sqrt))
        wt("v", cs, css2)
        VN(lambda v: v.tensor_scalar(slab[:], slab[:], 1e-12, None, OP.max))
        VD()
        VN(lambda v: v.reciprocal(slab[:], slab[:]))
        VD()
        cvr = VI(lambda v: v.tensor_scalar(slab[:], slab[:], float(np.float32(np.sqrt(1024.0))), None, OP.mult))
        wt("t", cv, cvr)
        for g in range(NTG):
            tb = TI(lambda t, g=g: t.matmul(psB[4][:, :], onesr_f[:],
                                            slab[0:1, g * 512:(g + 1) * 512], start=True, stop=True))
            wt("s", ct, tb)
            sbx = SI(lambda s, g=g: s.copy(bcA[:, g * 512:(g + 1) * 512], psB[4][:, :]))
            wt("t", cs, sbx)
        s_invrb = cs.n
        dbg_add("invr_b", bcA[:, :], (128, T))

    if phase_limit >= 4:
        sfree("xrot3", "xrot2", "xrot1", "xrot0")
        m2_bf = sbt("m2_bf", [1, T], BF16)
        c1_bf = sbt("c1_bf", [1, D], BF16)
        au = sbt("au", [128, NCH, T], BF16)
        u_p = sbt("u_p", [128, T], BF16)
        q_p = sbt("q_p", [128, T], BF16)
        k_p = sbt("k_p", [128, T], BF16)
        v_p = sbt("v_p", [128, NTT, 128], BF16)
        pt = sbt("pt", [128, 2, 2048], BF16)
        b1v_bf = sbt("b1v_bf", [1, D], BF16)
        wpair = [sbt(f"wpair{i}", [128, NCH, 512], BF16) for i in range(2)]
        WOFF = {"u": 0, "v": 128, "q": 256, "k": 384}
        qwp = [stream("qwp0"), stream("qwp1")]

        ldb1v = g_dma(stream("qrow"), b1v_bf[:], vrs("b1", 1024, 2048))
        wt("t", stream("qrow"), ldb1v)
        wt("t", cv, xn_ready)

        pair_done_t = {}
        W1v = W1.rearrange("(c p) n -> p c n", p=128)
        for p in range(8):
            sl = p % 2
            if p >= 2:
                wt("g", ct, pair_done_t[p - 2])
            g_dma(qwp[sl], wpair[sl][:], W1p[p, :, :, :])
            ldw = qwp[sl].n
            wt("t", qwp[sl], ldw)
            if p > 0:
                wt("s", cv, cv.n)
                wt("t", cs, cs.n)
            for oi, (nm, dst, bj) in enumerate((("u", u_p, p), ("q", q_p, 16 + p),
                                                ("k", k_p, 24 + p))):
                for g in range(NTG):
                    bank = (oi * NTG + g) % 2
                    if p + oi + g > 0:
                        wt("t", cs, proj_ev.get(bank, 0))
                    for c in range(NCH):
                        emitter = TN if c < NCH - 1 else TI
                        emitter(lambda t, sl=sl, o=WOFF[nm], c=c, g=g, bank=bank: t.matmul(
                            psB[bank][:, :], wpair[sl][:, c, o:o + 128],
                            xn[:, c, g * 512:(g + 1) * 512],
                            start=(c == 0), stop=(c == NCH - 1)))
                    tmm = ct.n
                    wt("s", ct, tmm)
                    proj_ev[bank] = SI(lambda s, dst=dst, g=g, bj=bj, bank=bank: s.activation(
                        dst[:, g * 512:(g + 1) * 512], psB[bank][:, :], AF.Silu,
                        bias=vcs("b1", bj)))
                    if phase_limit >= 2 and oi == 0 and g < 3:
                        next(rg, None)
            for tt in range(NTT):
                bank = 2 + tt % 2
                wt("t", cs, v_ev.get(bank, 0))
                for c in range(NCH):
                    TN(lambda t, sl=sl, c=c, tt=tt, bank=bank: t.matmul(
                        psB[bank][:, 0:128], xn[:, c, tt * 128:(tt + 1) * 128],
                        wpair[sl][:, c, 128:256], start=(c == 0), stop=False))
                tvm = TI(lambda t, sl=sl, tt=tt, bank=bank, p=p: t.matmul(
                    psB[bank][:, 0:128], onesr_bf[:], b1v_bf[0:1, p * 128:(p + 1) * 128],
                    start=False, stop=True))
                wt("s", ct, tvm)
                v_ev[bank] = SI(lambda s, tt=tt, bank=bank: s.activation(
                    v_p[:, tt, :], psB[bank][:, 0:128], AF.Silu))
            pair_done_t[p] = ct.n
            proj_all = cs.n
            wt("t", cs, proj_all)
            for g in range(NTG):
                for grp in range(8):
                    ptsl = grp % 2
                    t_half = {}
                    for tkk in range(2):
                        tk = 2 * grp + tkk
                        wt("t", cs, sc_ev.get("A" if tkk == 0 else "B", 0))
                        for h in range(2):
                            bank = 2 * tkk + h
                            emitter = TI if h == 1 else TN
                            emitter(lambda t, tk=tk, g=g, h=h, bank=bank: t.matmul(
                                psB[bank][:, :], k_p[64 * h:64 * h + 64, tk * 128:(tk + 1) * 128],
                                q_p[64 * h:64 * h + 64, g * 512:(g + 1) * 512],
                                start=True, stop=True, tile_position=(64 * h, 0)))
                        t_half[tkk] = ct.n
                    t_sc = ct.n
                    if grp > 0:
                        pgrp = grp - 1
                        if grp == 1:
                            wt("t", cv, au_ev.get(6, 0))
                        for tkk in range(2):
                            tk = 2 * pgrp + tkk
                            TN(lambda t, tk=tk, psl=1 - ptsl, tkk=tkk: t.matmul(
                                psB[6][0:64, :], v_p[:, tk, 0:64],
                                pt[:, psl, (2 * tkk) * 512:(2 * tkk + 1) * 512],
                                start=(tk == 0), stop=False, tile_position=(0, 0)))
                            emitter = TI if tkk == 1 else TN
                            emitter(lambda t, tk=tk, psl=1 - ptsl, tkk=tkk: t.matmul(
                                psB[7][64:128, :], v_p[:, tk, 64:128],
                                pt[:, psl, (2 * tkk + 1) * 512:(2 * tkk + 2) * 512],
                                start=(tk == 0), stop=False,
                                tile_position=(0, 64), skip_group_check=True))
                        av_mm[1 - ptsl] = ct.n
                        if grp == 2 and phase_limit >= 2:
                            next(rg, None)
                    wt("s", ct, t_half[0])
                    wt("s", ct, av_mm.get(ptsl, 0))
                    sc_ev["A"] = SI(lambda s, ptsl=ptsl: s.activation(
                        pt[:, ptsl, 0:1024], psALL[:, 0:2, :], AF.Silu))
                    wt("s", ct, t_sc)
                    sc_ev["B"] = SI(lambda s, ptsl=ptsl: s.activation(
                        pt[:, ptsl, 1024:2048], psALL[:, 2:4, :], AF.Silu))
                ptsl = 7 % 2
                wt("t", cs, sc_ev["B"])
                for tkk in range(2):
                    tk = 14 + tkk
                    TN(lambda t, tk=tk, ptsl=ptsl, tkk=tkk: t.matmul(
                        psB[6][0:64, :], v_p[:, tk, 0:64],
                        pt[:, ptsl, (2 * tkk) * 512:(2 * tkk + 1) * 512],
                        start=False, stop=(tkk == 1), tile_position=(0, 0)))
                    emitter = TI if tkk == 1 else TN
                    emitter(lambda t, tk=tk, ptsl=ptsl, tkk=tkk: t.matmul(
                        psB[7][64:128, :], v_p[:, tk, 64:128],
                        pt[:, ptsl, (2 * tkk + 1) * 512:(2 * tkk + 2) * 512],
                        start=False, stop=(tkk == 1),
                        tile_position=(0, 64), skip_group_check=True))
                av_mm[ptsl] = ct.n
                wt("v", ct, av_mm[ptsl])
                VN(lambda v, p=p, g=g: v.scalar_tensor_tensor(
                    au[0:64, p, g * 512:(g + 1) * 512], psB[6][0:64, :], 1.0 / 2048.0,
                    u_p[0:64, g * 512:(g + 1) * 512], OP.mult, OP.mult))
                au_ev[6] = VI(lambda v, p=p, g=g: v.scalar_tensor_tensor(
                    au[64:128, p, g * 512:(g + 1) * 512], psB[7][64:128, :], 1.0 / 2048.0,
                    u_p[64:128, g * 512:(g + 1) * 512], OP.mult, OP.mult))

        dbg_add("au", au[:, :, :], (128, NCH * T), BF16)

    if phase_limit >= 5:
        if phase_limit >= 2:
            for _ in rg:
                pass
        sfree("wpair1", "wpair0", "b1v_bf", "pt", "v_p", "k_p", "q_p", "u_p")
        W2sb = sbt("W2sb", [128, NCH, D], BF16)
        beta_bf = sbt("beta_bf", [128, 8], BF16)
        c2b_col = sbt("c2b_col", [128, 8], F32)
        qw2 = stream("qw2")
        wt("g", ct, ct.n)
        wt("g", cv, cv.n)
        g_dma(qw2, W2sb[:], W2d.rearrange("(c p) n -> p c n", p=128))
        ldw2 = qw2.n
        cvbeta = VI(lambda v: v.tensor_copy(beta_bf[:], vcol[:, VC["bt"]:VC["bt"] + 8]))
        wt("t", qw2, ldw2)
        wt("t", cv, cv.n)
        wt("t", cs, cs.n)
        for h in range(2):
            for c in range(NCH):
                emitter = TN if c < NCH - 1 else TI
                emitter(lambda t, h=h, c=c: t.matmul(
                    psB[2 + h][0:1, 0:512], beta_bf[:, c:c + 1],
                    W2sb[:, c, h * 512:(h + 1) * 512], start=(c == 0), stop=(c == NCH - 1)))
        t_c2 = ct.n
        wt("s", ct, t_c2)
        SN(lambda s: s.copy(slab2[0:1, 0:512], psB[2][0:1, 0:512]))
        s_c2 = SI(lambda s: s.copy(slab2[0:1, 512:1024], psB[3][0:1, 0:512]))
        wt("v", ct, t_c2)
        for c in range(NCH):
            emitter = VN if c < NCH - 1 else VI
            emitter(lambda v, c=c: v.tensor_scalar(W2sb[:, c, :], W2sb[:, c, :],
                                                   vcs("gm", c), 0.0, OP.mult, OP.add))
        cv_gfold = cv.n
        wt("y", cv, cv.n)
        ldb2 = y_dma(stream("qrow2"), sqrot[1][0:1, 0:1024], vrs("b2", 0, 1024))
        wt("v", stream("qrow2"), ldb2)
        wt("v", cs, s_c2)
        cv_c2b = VI(lambda v: v.tensor_tensor(slab2[0:1, 0:1024], slab2[0:1, 0:1024],
                                              sqrot[1][0:1, 0:1024], OP.add))
        wt("t", cv, cv_c2b)
        for oc in range(NCH):
            ttr = TI(lambda t, oc=oc: t.transpose(psB[4][0:128, 0:1],
                                                  slab2[0:1, oc * 128:(oc + 1) * 128], eye[0:1, 0:1]))
            wt("s", ct, ttr)
            str_ = SI(lambda s, oc=oc: s.copy(c2b_col[:, oc:oc + 1], psB[4][0:128, 0:1]))
            wt("t", cs, str_)
        wt("t", cv, cv_gfold)
        wt("t", cs, s_c2)
        for h in range(2):
            for c in range(NCH):
                emitter = TN if c < NCH - 1 else TI
                emitter(lambda t, h=h, c=c: t.matmul(
                    psB[2 + h][0:1, 0:512], onesc_bf[:], W2sb[:, c, h * 512:(h + 1) * 512],
                    start=(c == 0), stop=(c == NCH - 1)))
        t_c1 = ct.n
        wt("s", ct, t_c1)
        SN(lambda s: s.copy(c1_bf[0:1, 0:512], psB[2][0:1, 0:512]))
        s_c1 = SI(lambda s: s.copy(c1_bf[0:1, 512:1024], psB[3][0:1, 0:512]))
        VD()
        wt("t", cs, s_c1)
        au_st = {}
        for c in range(NCH):
            sl = c % 2
            if c >= 2:
                wt("v", ct, au_st[c - 2])
            cvq2 = VI(lambda v, sl=sl, c=c: v.tensor_tensor(
                sqrot[sl][:], au[:, c, :], au[:, c, :], OP.mult))
            wt("t", cv, cvq2)
            for g in range(NTG):
                TN(lambda t, c=c, g=g: t.matmul(
                    psB[g][0:1, 0:512], onesc_bf[:], au[:, c, g * 512:(g + 1) * 512],
                    start=(c == 0), stop=(c == NCH - 1)))
            for g in range(NTG):
                emitter = TN if g < NTG - 1 else TI
                emitter(lambda t, sl=sl, c=c, g=g: t.matmul(
                    psB[4 + g][0:1, 0:512], onesc_f, sqrot[sl][:, g * 512:(g + 1) * 512],
                    start=(c == 0), stop=(c == NCH - 1)))
            au_st[c] = ct.n
        wt("s", ct, au_st[NCH - 1])
        for g in range(NTG):
            SN(lambda s, g=g: s.copy(slab[0:1, g * 512:(g + 1) * 512], psB[g][0:1, 0:512]))
            emitter = SN if g < NTG - 1 else SI
            emitter(lambda s, g=g: s.copy(slab2[0:1, g * 512:(g + 1) * 512], psB[4 + g][0:1, 0:512]))
        s_aust = cs.n
        wt("v", cs, s_aust)
        VN(lambda v: v.tensor_scalar(slab[:], slab[:], 1.0 / D, None, OP.mult))
        VD()
        VN(lambda v: v.tensor_scalar(slab2[:], slab2[:], 1.0 / D, None, OP.mult))
        VN(lambda v: v.tensor_tensor(sqrot[0][0:1, 0:T], slab[:], slab[:], OP.mult))
        VD()
        VN(lambda v: v.tensor_tensor(slab2[:], slab2[:], sqrot[0][0:1, 0:T], OP.subtract))
        VD()
        cvv2 = VI(lambda v: v.tensor_scalar(slab2[:], slab2[:], 1e-5, None, OP.add))
        wt("s", cv, cvv2)
        css3 = SI(lambda s: s.activation(slab2[:], slab2[:], AF.Sqrt))
        wt("v", cs, css3)
        VN(lambda v: v.reciprocal(slab2[:], slab2[:]))
        VN(lambda v: v.tensor_scalar(sqrot[1][0:1, 0:T], slab[:], -1.0, None, OP.mult))
        VD()
        cvm2 = VI(lambda v: v.tensor_copy(m2_bf[:], sqrot[1][0:1, 0:T]))
        wt("t", cv, cvm2)
        for g in range(NTG):
            tb = TI(lambda t, g=g: t.matmul(psB[4][:, :], onesr_f[:],
                                            slab2[0:1, g * 512:(g + 1) * 512], start=True, stop=True))
            wt("s", ct, tb)
            sbx = SI(lambda s, g=g: s.copy(bcB[:, g * 512:(g + 1) * 512], psB[4][:, :]))
            wt("t", cs, sbx)
        s_iaub = cs.n
        VD()
        wt("t", cs, max(s_c1, s_iaub))
        smix_ev = {}
        for g in range(NTG):
            for oc in range(NCH):
                bank = 2 + (g * NCH + oc) % 2
                wt("t", cv, smix_ev.get(bank, 0))
                for c in range(NCH):
                    TN(lambda t, c=c, g=g, oc=oc, bank=bank: t.matmul(
                        psB[bank][:, :], W2sb[:, c, oc * 128:(oc + 1) * 128],
                        au[:, c, g * 512:(g + 1) * 512], start=(c == 0), stop=False))
                tmm = TI(lambda t, g=g, oc=oc, bank=bank: t.matmul(
                    psB[bank][:, :], c1_bf[0:1, oc * 128:(oc + 1) * 128],
                    m2_bf[0:1, g * 512:(g + 1) * 512], start=False, stop=True))
                wt("v", ct, tmm)
                VN(lambda v, g=g, bank=bank: v.tensor_tensor(
                    sqrot[1][:, 0:512], psB[bank][:, :], bcB[:, g * 512:(g + 1) * 512], OP.mult))
                VD()
                smix_ev[bank] = VI(lambda v, g=g, oc=oc: v.tensor_scalar(
                    smix[:, oc, g * 512:(g + 1) * 512], sqrot[1][:, 0:512],
                    c2b_col[:, oc:oc + 1], 0.0, OP.add, OP.add))
                VD()
        dbg_add("smix0", smix[:, :, :], (128, NCH * T), BF16)

    if phase_limit >= 6:
        sfree("c2b_col", "beta_bf", "W2sb", "au", "c1_bf", "m2_bf")
        QT = 512
        NQ = T // QT
        w1l_b = [sbt(f"w1lb{i}", [128, NCH, 128], BF16) for i in range(2)]
        w2l_b = [sbt(f"w2lb{i}", [128, 4, 128], BF16) for i in range(2)]
        xng = sbt("xng", [128, NCH, QT], BF16)
        gl_q = sbt("gl_q", [128, 4, QT], BF16)
        ghr = [sbt(f"ghr{i}", [128, QT], BF16) for i in range(2)]
        ghbuf = sbt("ghbuf", [128, 32, QT], BF16)
        w1h_b = [sbt(f"w1hb{i}", [128, NCH, 128], BF16) for i in range(2)]
        w2h_b = [sbt(f"w2hb{i}", [128, 2, 1024], BF16) for i in range(2)]
        gt_b = [sbt(f"gtb{i}", [128, QT], BF16) for i in range(2)]
        sg_f = [sbt(f"sgf{i}", [128, QT], F32) for i in range(2)]
        sq6v = [bcB[:, 0:QT], bcB[:, QT:2 * QT]]
        xc_b = sbt("xcb", [128, 4, QT], F32)
        w1hS = dtn("w1hS", [32, 128, 1024], BF16, "Internal")
        w2hS = dtn("w2hS", [16, 128, 2048], BF16, "Internal")
        ql = [stream("ql0"), stream("ql1")]
        ql2 = [stream("ql20"), stream("ql21")]
        qh1 = [stream("qh10"), stream("qh11")]
        qh2 = [stream("qh20"), stream("qh21")]
        qstg = stream("qstg")
        qxc = [stream("qxc0"), stream("qxc1")]
        qsp = [stream("qsp0"), stream("qsp1")]

        def y_sdma(st, out, in_):
            st.n += 16
            Q["y"].append(lambda e, o=out, i=in_, h=st.h: e.dma_start(out=o, in_=i).then_inc(h, 16))
            return st.n

        w2hv = w2hT.rearrange("(c p) n -> p c n", p=128)
        xv0 = xT.rearrange("(c p) t -> p c t", p=128)
        xspv = xsp.rearrange("(c p) t -> p c t", p=128)
        ldb2h = g_dma(stream("qrow"), b2h_bf[:], vrs("b2h", 0, 1024))
        wt("g", ct, ct.n)
        wt("g", cv, cv.n)
        wt("v", ct, ct.n)
        wt("y", ct, ct.n)
        wt("y", cv, cv.n)
        wt("t", stream("qrow"), stream("qrow").n)
        wg = {}
        h1ev = {}
        stat_t = {}
        glq_ev = {}
        maskv = {}
        o_ev = {}
        xng_guard = 0
        stg_n = {}
        for Qi in range(NQ):
            Qs = slice(Qi * QT, (Qi + 1) * QT)
            wt("y", cv, o_ev.get("xcA", 0))
            for oc in range(4):
                y_dma(qxc[0], xc_b[:, oc, :], xv0[:, oc, Qs])
            xcA = qxc[0].n
            wt("v", ct, xng_guard)
            for c in range(NCH):
                emitter = VN if c < NCH - 1 else VI
                emitter(lambda v, c=c, Qs=Qs: v.scalar_tensor_tensor(
                    xng[:, c, :], xn[:, c, Qs], vcs("gl", c), bcA[:, Qs], OP.mult, OP.mult))
            cv_xngl = cv.n
            wt("t", cv, cv_xngl)
            for m in range(4):
                sl = m % 2
                bank = m % 2
                wt("g", ct, wg.get(("l", sl), 0))
                lw = g_dma(ql[sl], w1l_b[sl][:], w1lt[m, :, :])
                wt("t", ql[sl], lw)
                wt("t", cs, glq_ev.get(bank, 0))
                for c in range(NCH):
                    emitter = TN if c < NCH - 1 else TI
                    emitter(lambda t, m=m, c=c, sl=sl, bank=bank: t.matmul(
                        psB[bank][:, :], w1l_b[sl][:, c, :], xng[:, c, :],
                        start=(c == 0), stop=(c == NCH - 1)))
                tmm = ct.n
                wg[("l", sl)] = tmm
                wt("s", ct, tmm)
                glq_ev[bank] = SI(lambda s, m=m, bank=bank: s.activation(
                    gl_q[:, m, :], psB[bank][:, :], AF.Gelu, bias=vcs("b1l", m)))
            light_done = ct.n
            wt("v", ct, light_done)
            for c in range(NCH):
                emitter = VN if c < NCH - 1 else VI
                emitter(lambda v, c=c, Qs=Qs: v.scalar_tensor_tensor(
                    xng[:, c, :], xn[:, c, Qs], vcs("gh", c), bcA[:, Qs], OP.mult, OP.mult))
            cv_xngh = cv.n
            wt("t", cv, cv_xngh)
            for ht in range(32):
                sl = ht % 2
                bank = 2 + ht % 2
                if Qi == 0:
                    wt("g", ct, wg.get(("h1", sl), 0))
                    wt("g", stream(f"qstg1{sl}"), stg_n.get(("h1", sl), 0))
                    lw = g_dma(qh1[sl], w1h_b[sl][:], w1ht[ht, :, :])
                    wt("t", qh1[sl], lw)
                    wt("y", qh1[sl], lw)
                    stg_n[("h1", sl)] = y_sdma(stream(f"qstg1{sl}"), w1hS[ht, :, :],
                                               w1h_b[sl][:].rearrange("p c n -> p (c n)"))
                else:
                    wt("y", ct, wg.get(("h1", sl), 0))
                    wt("y", stream(f"qstg1{sl}"), stg_n.get(("h1", sl), 0))
                    lw = y_dma(stream(f"qh1s{sl}"), w1h_b[sl][:], w1hS[ht, :, :])
                    wt("t", stream(f"qh1s{sl}"), lw)
                wt("t", cs, h1ev.get(bank, 0))
                for c in range(NCH):
                    emitter = TN if c < NCH - 1 else TI
                    emitter(lambda t, ht=ht, c=c, sl=sl, bank=bank: t.matmul(
                        psB[bank][:, :], w1h_b[sl][:, c, :], xng[:, c, :],
                        start=(c == 0), stop=(c == NCH - 1)))
                tmm = ct.n
                wg[("h1", sl)] = tmm
                wt("s", ct, tmm)
                wt("s", cv, maskv.get(("slot", ht % 2), 0))
                sgh = SI(lambda s, ht=ht, sl=ht % 2, bank=bank: s.activation(
                    ghr[sl][:], psB[bank][:, :], AF.Gelu, bias=vcs("b1h", ht)))
                h1ev[bank] = sgh
                wt("v", cs, sgh)
                maskv[("slot", ht % 2)] = maskv[ht] = VI(
                    lambda v, ht=ht, sl=ht % 2, Qs=Qs: v.tensor_tensor(
                        ghbuf[:, ht, :], ghr[sl][:], selb[:, Qs], OP.mult))
            xng_guard = ct.n

            wt("t", cs, cs.n)
            for oc in range(NCH):
                sl = oc % 2
                wt("g", ct, wg.get(("l2", sl), 0))
                lw = g_dma(ql2[sl], w2l_b[sl][:], w2lt[oc, :, :])
                wt("t", ql2[sl], lw)
                if Qi > 0:
                    wt("t", cv, o_ev.get(oc, 0))
                for c2 in range(4):
                    emitter = TN if c2 < 3 else TI
                    emitter(lambda t, oc=oc, c2=c2, sl=sl: t.matmul(
                        psB[oc][:, :], w2l_b[sl][:, c2, :], gl_q[:, c2, :],
                        start=(c2 == 0), stop=False))
                wg[("l2", sl)] = ct.n
            for hb in range(16):
                sl = hb % 2
                if Qi == 0:
                    wt("g", ct, wg.get(("h2", sl), 0))
                    wt("g", stream(f"qstg2{sl}"), stg_n.get(("h2", sl), 0))
                    lw = g_dma(qh2[sl], w2h_b[sl][:], w2hv[:, hb * 2:(hb + 1) * 2, :])
                    wt("t", qh2[sl], lw)
                    wt("y", qh2[sl], lw)
                    stg_n[("h2", sl)] = y_sdma(stream(f"qstg2{sl}"), w2hS[hb, :, :],
                                               w2h_b[sl][:].rearrange("p c n -> p (c n)"))
                else:
                    wt("y", ct, wg.get(("h2", sl), 0))
                    wt("y", stream(f"qstg2{sl}"), stg_n.get(("h2", sl), 0))
                    lw = y_dma(stream(f"qh2s{sl}"), w2h_b[sl][:], w2hS[hb, :, :])
                    wt("t", stream(f"qh2s{sl}"), lw)
                for hq in range(2):
                    ht = hb * 2 + hq
                    wt("t", cv, maskv[ht])
                    for oc in range(NCH):
                        emitter = TN if oc < NCH - 1 else TI
                        emitter(lambda t, oc=oc, ht=ht, hq=hq, sl=sl: t.matmul(
                            psB[oc][:, :], w2h_b[sl][:, hq, oc * 128:(oc + 1) * 128],
                            ghbuf[:, ht, :], start=False, stop=False))
                wg[("h2", sl)] = ct.n
            for oc in range(NCH):
                emitter = TN if oc < NCH - 1 else TI
                emitter(lambda t, oc=oc, Qi=Qi: t.matmul(
                    psB[oc][:, :], b2h_bf[0:1, oc * 128:(oc + 1) * 128],
                    sel_bf[0:1, Qi * QT:(Qi + 1) * QT], start=False, stop=True))
            t_oend = ct.n
            xcB = [0]
            stat_mm = {}
            for oc in range(NCH):
                slc = oc % 2
                wt("s", ct, t_oend)
                wt("s", cv, o_ev.get(("gt", slc), 0))
                sgt = SI(lambda s, oc=oc, slc=slc: s.activation(
                    gt_b[slc][:], psB[oc][:, :], AF.Silu, bias=vcs("b2l", oc)))
                wt("v", cs, sgt)
                wt("v", qsp[slc], stg_n.get(("sp", slc), 0))
                wt("v", ct, stat_t.get(slc, 0))
                VN(lambda v, oc=oc, slc=slc, Qs=Qs: v.tensor_tensor(
                    sg_f[slc][:], gt_b[slc][:], smix[:, oc, Qs], OP.mult))
                VD()
                wt("v", qxc[0] if oc < 4 else qxc[1], xcA if oc < 4 else xcB[0])
                cvy = VI(lambda v, oc=oc, slc=slc: v.tensor_tensor(
                    sg_f[slc][:], sg_f[slc][:], xc_b[:, oc % 4, :], OP.add))
                VD()
                cvsq6 = VI(lambda v, oc=oc, slc=slc: v.tensor_tensor(
                    sq6v[slc], sg_f[slc][:], sg_f[slc][:], OP.mult))
                stat_mm[oc] = (cvy, cvsq6, slc)
                o_ev[("gt", slc)] = cvsq6
                o_ev[oc] = cvy
                if oc == 3:
                    o_ev["xcA"] = cvy
                    wt("y", cv, cvy)
                    for oc2 in range(4):
                        y_dma(qxc[1], xc_b[:, oc2, :], xv0[:, 4 + oc2, Qs])
                    xcB[0] = qxc[1].n
                if oc == 7:
                    o_ev["xcB"] = cvy
                wt("y", cv, cvy)
                stg_n[("sp", slc)] = y_sdma(qsp[slc], xspv[:, oc, Qs], sg_f[slc][:])
                if oc >= 1:
                    pcv, pcq, pslc = stat_mm[oc - 1]
                    wt("t", cv, pcq)
                    if oc == 1:
                        wt("t", cs, sgt)
                    TN(lambda t, pslc=pslc, oc=oc: t.matmul(
                        psB[0][0:1, 0:512], onesc_f, sg_f[pslc][:],
                        start=(oc == 1), stop=False))
                    tq6 = TI(lambda t, pslc=pslc, oc=oc: t.matmul(
                        psB[1][0:1, 0:512], onesc_f, sq6v[pslc],
                        start=(oc == 1), stop=False))
                    stat_t[pslc] = tq6
            pcv, pcq, pslc = stat_mm[7]
            wt("t", cv, pcq)
            TN(lambda t, pslc=pslc: t.matmul(psB[0][0:1, 0:512], onesc_f, sg_f[pslc][:],
                                             start=False, stop=True))
            tst = TI(lambda t, pslc=pslc: t.matmul(psB[1][0:1, 0:512], onesc_f,
                                                   sq6v[pslc], start=False, stop=True))
            stat_t[pslc] = tst
            wt("s", ct, tst)
            SN(lambda s, Qs=Qs: s.copy(slab[0:1, Qs], psB[0][0:1, 0:512]))
            s_st6 = SI(lambda s, Qs=Qs: s.copy(slab2[0:1, Qs], psB[1][0:1, 0:512]))
            o_ev[0] = max(o_ev[0], 0)
            wt("t", cs, s_st6)
        sp_total = qst.n

    if phase_limit >= 7:
        sfree("xcb", "sgf1", "sgf0", "gtb1", "gtb0",
              "w2hb1", "w2hb0", "w1hb1", "w1hb0", "ghbuf", "ghr1", "ghr0",
              "gl_q", "xng", "w2lb1", "w2lb0", "w1lb1", "w1lb0",
              "b2h_bf", "selb", "sel_bf")
        xrot = [sbt(f"x2rot{i}", [128, T], F32) for i in range(4)]
        wt("y", qsp[0], qsp[0].n)
        wt("y", qsp[1], qsp[1].n)
        wt("y", ct, ct.n)
        wt("y", cs, cs.n)
        wt("y", cv, cv.n)
        wt("t", cs, cs.n)
        wt("v", cs, cs.n)
        wt("v", qsp[0], qsp[0].n)
        wt("v", qsp[1], qsp[1].n)
        emit_ln(xsp, with_s=False, stats_ready=True)
        dbg_add("xn1", xn[:, :, :], (128, NCH * T), BF16)

    if phase_limit >= 8:
        sfree("x2rot3", "x2rot2", "x2rot1", "x2rot0")
        scC = {w: sbt(f"scC_{w}", [128, 8], F32) for w in ("q", "k", "v")}
        bvp_bf = sbt("bvp_bf", [1, D], BF16)
        wqi = {w: sbt(f"wqi_{w}", [128, NCH, D], BF16) for w in ("q", "k", "v")}
        iq_t = sbt("iq_t", [128, D], I32)
        qoi = [stream("qoi0"), stream("qoi1")]
        qio = [stream("qio0"), stream("qio1")]
        recip_b = bcA[:, 0:D]
        sc_b = bcA[:, D:2 * D]
        wt("y", ct, ct.n)
        wt("y", cv, cv.n)
        wt("g", ct, ct.n)
        wt("g", cv, cv.n)
        quant_v = {}
        WS = ("q", "k", "v")
        rb = {"q": bcA[:, 0:D], "k": bcB[:, 0:D], "v": bcA[:, 0:D]}

        def issue_am(w, c, guard):
            sl = c % 2
            if guard:
                wt("g", cv, guard)
            quant_v[(w, "am_ld", c)] = g_dma(
                qoi[sl], sqrot[0][:, sl * 1024:(sl + 1) * 1024],
                WO[w].rearrange("(c p) n -> p c n", p=128)[:, c, :])

        def emit_reduce(w, wi, c):
            sl = c % 2
            wt("v", qoi[sl], quant_v[(w, "am_ld", c)])
            r = VI(lambda v, w=w, c=c, sl=sl: v.tensor_reduce(
                scC[w][:, c:c + 1], sqrot[0][:, sl * 1024:(sl + 1) * 1024],
                axis=AX.X, op=OP.max, apply_absolute_value=True))
            VD()
            if c + 2 < NCH:
                issue_am(w, c + 2, r)
            elif wi < 2:
                issue_am(WS[wi + 1], c + 2 - NCH, r)

        def emit_dance(w):
            VN(lambda v, w=w: v.tensor_scalar(scC[w][:], scC[w][:], 1.0 / 127.0, 1e-8,
                                              OP.mult, OP.max))
            VD()
            cvrc = VI(lambda v, w=w: v.reciprocal(col["junk"][:, 0:8], scC[w][:]))
            wt("t", cv, cvrc)
            for oc in range(NCH):
                emitter = TN if oc < NCH - 1 else TI
                emitter(lambda t, oc=oc: t.transpose(
                    psB[4 + oc // 4][0:1, (oc % 4) * 128:(oc % 4) * 128 + 128],
                    col["junk"][:, oc:oc + 1], eye))
            ttr = ct.n
            wt("s", ct, ttr)
            SN(lambda s: s.copy(slab[0:1, 0:512], psB[4][0:1, 0:512]))
            s_tr = SI(lambda s: s.copy(slab[0:1, 512:1024], psB[5][0:1, 0:512]))
            wt("t", cs, s_tr)
            if w == "v":
                for oc in range(NCH):
                    emitter = TN if oc < NCH - 1 else TI
                    emitter(lambda t, oc=oc: t.transpose(
                        psB[4 + oc // 4][0:1, (oc % 4) * 128:(oc % 4) * 128 + 128],
                        scC["v"][:, oc:oc + 1], eye))
                ttr2 = ct.n
                wt("s", ct, ttr2)
                SN(lambda s: s.copy(slab2[0:1, 0:512], psB[4][0:1, 0:512]))
                s_tr2 = SI(lambda s: s.copy(slab2[0:1, 512:1024], psB[5][0:1, 0:512]))
                wt("t", cs, s_tr2)
            s_rows = cs.n
            wt("t", cs, s_rows)
            for h2 in range(2):
                tb = TI(lambda t, h2=h2: t.matmul(psB[6][:, :], onesr_f[:],
                                                  slab[0:1, h2 * 512:(h2 + 1) * 512],
                                                  start=True, stop=True))
                wt("s", ct, tb)
                sbx = SI(lambda s, w=w, h2=h2: s.copy(rb[w][:, h2 * 512:(h2 + 1) * 512],
                                                      psB[6][:, :]))
                wt("t", cs, sbx)
            if w == "v":
                for h2 in range(2):
                    tb = TI(lambda t, h2=h2: t.matmul(psB[7][:, :], onesr_f[:],
                                                      slab2[0:1, h2 * 512:(h2 + 1) * 512],
                                                      start=True, stop=True))
                    wt("s", ct, tb)
                    sbx = SI(lambda s, h2=h2: s.copy(sc_b[:, h2 * 512:(h2 + 1) * 512], psB[7][:, :]))
                    wt("t", cs, sbx)
                wt("y", cv, cv.n)
                ldbv = y_dma(stream("qrow3"), slab[0:1, 1024:1024 + D], vrs("bv", 0, D))
                wt("v", stream("qrow3"), ldbv)
                wt("v", cs, cs.n)
                VN(lambda v: v.tensor_tensor(slab[0:1, 1024:1024 + D], slab[0:1, 1024:1024 + D],
                                             slab[0:1, 0:D], OP.mult))
                VD()
                cvbvp = VI(lambda v: v.tensor_copy(bvp_bf[:], slab[0:1, 1024:1024 + D]))
                wt("t", cv, cvbvp)
            return cs.n

        sbc = {}

        def emit_io_chunk(w, c):
            sl = c % 2
            wi0 = WS.index(w)
            if c >= 2:
                wt("y", cv, quant_v.get((w, "io", c - 2), 0))
            elif wi0 > 0:
                wt("y", cv, quant_v.get((WS[wi0 - 1], "io", NCH - 2 + c), 0))
            else:
                wt("y", cv, cv.n)
            li = y_dma(qio[sl], sqrot[1][:, sl * 1024:(sl + 1) * 1024],
                       WT[w].rearrange("(c p) n -> p c n", p=128)[:, c, :])
            if c == 0:
                wt("v", cs, sbc[w])
            wt("v", qio[sl], li)
            VN(lambda v, w=w, sl=sl: v.tensor_tensor(
                sqrot[1][:, sl * 1024:(sl + 1) * 1024],
                sqrot[1][:, sl * 1024:(sl + 1) * 1024], rb[w], OP.mult))
            VD()
            VN(lambda v, sl=sl: v.tensor_copy(iq_t[:],
                                              sqrot[1][:, sl * 1024:(sl + 1) * 1024]))
            VD()
            quant_v[(w, "io", c)] = VI(lambda v, w=w, c=c: v.tensor_scalar(
                wqi[w][:, c, :], iq_t[:], -127.0, 127.0, OP.max, OP.min))
            VD()

        q_p2 = sbt("q_p2", [128, T], BF16)
        k_p2 = sbt("k_p2", [128, T], BF16)
        v_p2 = sbt("v_p2", [128, NTT, 128], BF16)
        qsq2 = sbt("qsq2", [128, T], BF16)
        pt2 = sbt("pt2", [128, 4, 2048], BF16)
        recipb_s = sbt("recipb_s", [128, 512], F32)
        aev = {}
        sc2_ev = {}
        pt2_mm = {}
        pend = [None]

        def emit_proj_mms(p, w, g, bank, first_gate=0):
            if first_gate:
                wt("t", cv, first_gate)
            wt("t", cv, aev.get(bank, 0))
            for c in range(NCH):
                emitter = TN if c < NCH - 1 else TI
                emitter(lambda t, w=w, c=c, g=g, p=p, bank=bank: t.matmul(
                    psB[bank][:, :], wqi[w][:, c, p * 128:(p + 1) * 128],
                    xn[:, c, g * 512:(g + 1) * 512],
                    start=(c == 0), stop=(c == NCH - 1)))
            return ct.n

        def emit_proj_evac(p, w, g, bank, tmm=None):
            dst = q_p2 if w == "q" else k_p2
            bnm = "bq" if w == "q" else "bk"
            if tmm:
                wt("v", ct, tmm)
            aev[bank] = VI(lambda v, w=w, dst=dst, g=g, p=p, bank=bank, bnm=bnm:
                           v.tensor_scalar(dst[:, g * 512:(g + 1) * 512],
                                           psB[bank][:, :], scC[w][:, p:p + 1],
                                           vcs(bnm, p), OP.mult, OP.add))

        def emit_proj_qk(p, w, g):
            bank = g % 2
            tmm = emit_proj_mms(p, w, g, bank)
            emit_proj_evac(p, w, g, bank, tmm)

        issue_am("q", 0, 0)
        issue_am("q", 1, 0)
        for c in range(NCH):
            emit_reduce("q", 0, c)
        sbc["q"] = emit_dance("q")
        tq_last = 0
        for wi in (1, 2):
            wp, wn = WS[wi - 1], WS[wi]
            for c in range(NCH):
                emit_io_chunk(wp, c)
                emit_reduce(wn, wi, c)
                if wi == 2 and c % 2 == 1:
                    tq_last = emit_proj_mms(
                        0, "q", c // 2, c // 2,
                        first_gate=quant_v[("q", "io", NCH - 1)] if c == 1 else 0)
            if wi == 2:
                for g in range(NTG):
                    emit_proj_evac(0, "q", g, g, tq_last if g == 0 else None)
            sbc[wn] = emit_dance(wn)
        tk_last = 0
        for c in range(NCH):
            emit_io_chunk("v", c)
            if c % 2 == 1:
                tk_last = emit_proj_mms(
                    0, "k", c // 2, c // 2,
                    first_gate=quant_v[("k", "io", NCH - 1)] if c == 1 else 0)
        for g in range(NTG):
            emit_proj_evac(0, "k", g, g, tk_last if g == 0 else None)
        wt("t", cv, cv.n)

        def emit_ctx(pgrp, psl, par):
            b0, b1 = (6, 7) if par == 0 else (7, 6)
            for h in range(2):
                for tkk in range(2):
                    tk = 2 * pgrp + tkk
                    off = (2 * h + tkk) * 512
                    last = (h == 1 and tkk == 1)
                    emitter = TI if last else TN
                    if h == 0:
                        emitter(lambda t, tk=tk, psl=psl, off=off, b=b0: t.matmul(
                            psB[b][0:64, :], v_p2[:, tk, 0:64],
                            pt2[:, psl, off:off + 512],
                            start=(tk == 0), stop=(tk == NTT - 1),
                            tile_position=(0, 0), skip_group_check=True))
                    else:
                        emitter(lambda t, tk=tk, psl=psl, off=off, b=b1: t.matmul(
                            psB[b][64:128, :], v_p2[:, tk, 64:128],
                            pt2[:, psl, off:off + 512],
                            start=(tk == 0), stop=(tk == NTT - 1),
                            tile_position=(0, 64), skip_group_check=True))

        def emit_norm():
            pp, gg, par, rc = pend[0]
            pend[0] = None
            b0, b1 = (6, 7) if par == 0 else (7, 6)
            wt("t", cv, rc)
            TI(lambda t: t.matmul(psB[4][:, :], onesr_f[:], sqrot[1][0:1, 0:512],
                                  start=True, stop=True))
            t_b1 = TI(lambda t: t.matmul(psB[5][:, :], onesr_f[:],
                                         sqrot[1][0:1, 512:1024], start=True, stop=True))
            wt("s", ct, t_b1)
            SN(lambda s: s.copy(recipb_s[0:64, :], psB[4][0:64, :]))
            s_rb = SI(lambda s: s.copy(recipb_s[64:128, :], psB[5][64:128, :]))
            sc2_ev["rb"] = s_rb
            wt("v", cs, s_rb)
            VN(lambda v, pp=pp, gg=gg, b=b0: v.tensor_tensor(
                smix[0:64, pp, gg * 512:(gg + 1) * 512], psB[b][0:64, :],
                recipb_s[0:64, :], OP.mult))
            aev[("ctx", par)] = VI(lambda v, pp=pp, gg=gg, b=b1: v.tensor_tensor(
                smix[64:128, pp, gg * 512:(gg + 1) * 512], psB[b][64:128, :],
                recipb_s[64:128, :], OP.mult))

        for p in range(8):
            if p > 0:
                wt("v", ct, ct.n)
                wt("t", cs, cs.n)
            if p > 0:
                for w in ("q", "k"):
                    for g in range(NTG):
                        emit_proj_qk(p, w, g)
            for tt in range(NTT):
                bank = 2 + tt % 2
                wt("t", cv, aev.get(bank, 0))
                for c in range(NCH):
                    TN(lambda t, c=c, tt=tt, p=p, bank=bank: t.matmul(
                        psB[bank][:, 0:128], xn[:, c, tt * 128:(tt + 1) * 128],
                        wqi["v"][:, c, p * 128:(p + 1) * 128],
                        start=(c == 0), stop=False))
                tvm = TI(lambda t, tt=tt, p=p, bank=bank: t.matmul(
                    psB[bank][:, 0:128], onesr_bf[:], bvp_bf[0:1, p * 128:(p + 1) * 128],
                    start=False, stop=True))
                wt("v", ct, tvm)
                aev[bank] = VI(lambda v, tt=tt, p=p, bank=bank: v.tensor_tensor(
                    v_p2[:, tt, :], psB[bank][:, 0:128],
                    sc_b[:, p * 128:(p + 1) * 128], OP.mult))
            proj_done = cv.n
            wt("v", ct, ct.n)
            cvq2a = VI(lambda v, p=p: v.tensor_tensor(qsq2[:], q_p2[:], q_p2[:], OP.mult))
            wt("t", cv, cvq2a)
            for g in range(NTG):
                emitter = TN if g < NTG - 1 else TI
                emitter(lambda t, g=g: t.matmul(
                    psB[g % 2][0:2, 0:512], ones2h_bf[:],
                    qsq2[:, g * 512:(g + 1) * 512], start=True, stop=True))
            tqn = ct.n
            wt("v", ct, tqn)
            for g in range(NTG):
                VN(lambda v, g=g: v.reduce_max(col["part"][0:2, 8 + g:9 + g],
                                               psB[g % 2][0:2, 0:512], axis=AX.X))
            VD()
            cvq2b = VI(lambda v, p=p: v.tensor_tensor(qsq2[:], k_p2[:], k_p2[:], OP.mult))
            wt("t", cv, cvq2b)
            for g in range(NTG):
                emitter = TN if g < NTG - 1 else TI
                emitter(lambda t, g=g: t.matmul(
                    psB[2 + g % 2][0:2, 0:512], ones2h_bf[:],
                    qsq2[:, g * 512:(g + 1) * 512], start=True, stop=True))
            tkn = ct.n
            wt("v", ct, tkn)
            for g in range(NTG):
                VN(lambda v, g=g: v.reduce_max(col["part"][0:2, 12 + g:13 + g],
                                               psB[2 + g % 2][0:2, 0:512], axis=AX.X))
            VD()
            VN(lambda v: v.reduce_max(col["part"][0:2, 4:5], col["part"][0:2, 8:12], axis=AX.X))
            cvmx = VI(lambda v: v.reduce_max(col["part"][0:2, 5:6], col["part"][0:2, 12:16], axis=AX.X))
            VD()
            wt("s", cv, cvmx)
            SN(lambda s: s.activation(col["part"][0:2, 4:5], col["part"][0:2, 4:5], AF.Sqrt))
            s_sq = SI(lambda s: s.activation(col["part"][0:2, 5:6], col["part"][0:2, 5:6], AF.Sqrt))
            wt("v", cs, s_sq)
            VN(lambda v: v.tensor_tensor(col["part"][0:2, 6:7], col["part"][0:2, 4:5],
                                         col["part"][0:2, 5:6], OP.mult))
            VD()
            cvnb = VI(lambda v: v.tensor_scalar(col["part"][0:2, 6:7], col["part"][0:2, 6:7],
                                                -0.125, None, OP.mult))
            wt("t", cv, cvnb)
            t_tb = TI(lambda t: t.transpose(psB[4][0:1, 0:2], col["part"][0:2, 6:7],
                                            eye[0:2, 0:2]))
            wt("s", ct, t_tb)
            s_tb = SI(lambda s: s.copy(tiny[0:1, 16:18], psB[4][0:1, 0:2]))
            wt("t", cs, s_tb)
            for h in range(2):
                tb = TI(lambda t, h=h: t.matmul(psB[4 + h][:, 0:1], onesr_f[:],
                                                tiny[0:1, 16 + h:17 + h], start=True, stop=True))
            wt("s", ct, ct.n)
            SN(lambda s: s.copy(col["tmpa"][:, 4:5], psB[4][:, 0:1]))
            s_nb = SI(lambda s: s.copy(col["tmpa"][:, 5:6], psB[5][:, 0:1]))
            wt("t", cs, s_nb)
            for g in range(NTG):
                par = (p * NTG + g) % 2
                for grp in range(8):
                    ptsl = grp % 4
                    t_half = {}
                    for h in range(2):
                        wt("t", cs, sc2_ev.get("A" if h == 0 else "B", 0))
                        for tkk in range(2):
                            tk = 2 * grp + tkk
                            bank = 2 * h + tkk
                            emitter = TI if tkk == 1 else TN
                            emitter(lambda t, tk=tk, g=g, h=h, bank=bank: t.matmul(
                                psB[bank][:, :], k_p2[64 * h:64 * h + 64, tk * 128:(tk + 1) * 128],
                                q_p2[64 * h:64 * h + 64, g * 512:(g + 1) * 512],
                                start=True, stop=True, tile_position=(64 * h, 0)))
                        t_half[h] = ct.n
                    t_sc = ct.n
                    if grp >= 2:
                        pgrp = grp - 2
                        psl = pgrp % 4
                        if grp == 2:
                            if pend[0] is not None:
                                emit_norm()
                            wt("t", cs, sc2_ev.get("rb", 0))
                            wt("t", cv, aev.get(("ctx", par), 0))
                        for h in range(2):
                            for tkk in range(2):
                                tk = 2 * pgrp + tkk
                                off = (2 * h + tkk) * 512
                                TN(lambda t, tk=tk, psl=psl, off=off, h=h: t.matmul(
                                    psB[4 + h][0:1, 0:512], onesc_bf[:],
                                    pt2[:, psl, off:off + 512],
                                    start=(tk == 0), stop=(tk == NTT - 1)))
                        emit_ctx(pgrp, psl, par)
                        pt2_mm[psl] = ct.n
                    wt("s", ct, t_half[0])
                    wt("s", ct, pt2_mm.get(ptsl, 0))
                    sc2_ev["A"] = SI(lambda s, ptsl=ptsl: s.activation(
                        pt2[:, ptsl, 0:1024], psALL[:, 0:2, :], AF.Exp,
                        bias=col["tmpa"][:, 4:5], scale=0.125))
                    wt("s", ct, t_sc)
                    sc2_ev["B"] = SI(lambda s, ptsl=ptsl: s.activation(
                        pt2[:, ptsl, 1024:2048], psALL[:, 2:4, :], AF.Exp,
                        bias=col["tmpa"][:, 5:6], scale=0.125))
                wt("t", cs, sc2_ev["B"])
                for pgrp in (6, 7):
                    psl = pgrp % 4
                    for h in range(2):
                        for tkk in range(2):
                            tk = 2 * pgrp + tkk
                            off = (2 * h + tkk) * 512
                            TN(lambda t, tk=tk, psl=psl, off=off, h=h: t.matmul(
                                psB[4 + h][0:1, 0:512], onesc_bf[:],
                                pt2[:, psl, off:off + 512],
                                start=(tk == 0), stop=(tk == NTT - 1)))
                for pgrp in (6, 7):
                    emit_ctx(pgrp, pgrp % 4, par)
                t_ctx = ct.n
                pt2_mm[2] = pt2_mm[3] = t_ctx
                wt("s", ct, t_ctx)
                SN(lambda s: s.copy(sqrot[1][0:1, 0:512], psB[4][0:1, 0:512]))
                s_rs = SI(lambda s: s.copy(sqrot[1][0:1, 512:1024], psB[5][0:1, 0:512]))
                wt("v", cs, s_rs)
                cv_rc = VI(lambda v: v.reciprocal(sqrot[1][0:1, 0:1024], sqrot[1][0:1, 0:1024]))
                pend[0] = (p, g, par, cv_rc)
        if pend[0] is not None:
            emit_norm()
        dbg_add("ctx1", smix[:, :, :], (128, NCH * T), BF16)

    if phase_limit >= 9:
        sfree("recipb_s", "pt2", "qsq2", "v_p2", "k_p2", "q_p2", "iq_t",
              "wqi_v", "wqi_k", "wqi_q", "bvp_bf", "scC_v", "scC_k", "scC_q",
              "sqrot1", "sqrot0", "bcB", "bcA", "slab2", "slab")
        QT2 = 512
        NQ2 = T // QT2
        hA = sbt("hA", [128, 32, QT2], BF16)
        hB = sbt("hB", [128, 32, QT2], BF16)
        NW = 4
        wsl12 = [sbt(f"w12b{i}", [128, 2, NCH, 128], BF16) for i in range(NW)]
        wslH = [sbt(f"wHb{i}", [128, 32, 128], BF16) for i in range(NW)]
        t1f = [sbt(f"t1f{i}", [128, QT2], F32) for i in range(2)]
        gt2 = [sbt(f"gt2{i}", [128, QT2], BF16) for i in range(2)]
        sg2 = [sbt(f"sg2{i}", [128, QT2], F32) for i in range(2)]
        xc2 = sbt("xc2", [128, NCH, QT2], F32)
        q12 = [stream(f"q12_{i}") for i in range(NW)]
        qH = [stream(f"qH{i}") for i in range(NW)]
        qx2 = stream("qx2")
        qout = [stream("qout0"), stream("qout1")]
        wdma = lambda i: g_dma

        yv = yTo.rearrange("(c p) t -> p c t", p=128)
        wt("g", ct, ct.n)
        wt("g", cv, cv.n)
        wt("y", ct, ct.n)
        wt("y", cv, cv.n)
        h_ev = {}
        w_use = {}
        out_use = {}
        for Qi in range(NQ2):
            Qs = slice(Qi * QT2, (Qi + 1) * QT2)
            wt("y", cv, out_use.get(0, 0))
            wt("y", cv, out_use.get(1, 0))
            for oc in range(NCH):
                y_dma(qx2, xc2[:, oc, :], xspv[:, oc, Qs])
            xq_tot = qx2.n
            for ht in range(32):
                slh = ht % NW
                wt("g", ct, w_use.get(("12", slh), 0))
                wdma(slh)(q12[slh], wsl12[slh][:, 0, :, :], w1wt[ht, :, :])
                lw = wdma(slh)(q12[slh], wsl12[slh][:, 1, :, :], w2wt[ht, :, :])
                wt("t", q12[slh], lw)
                bankA = ht % 2
                bankB = 2 + ht % 2
                wt("t", cs, h_ev.get(("t1", ht % 2), 0))
                wt("t", cv, h_ev.get(("h", ht % 2), 0))
                for c in range(NCH):
                    TN(lambda t, c=c, slh=slh, bankA=bankA, Qs=Qs: t.matmul(
                        psB[bankA][:, :], wsl12[slh][:, 0, c, :], xn[:, c, Qs],
                        start=(c == 0), stop=(c == NCH - 1)))
                for c in range(NCH):
                    emitter = TN if c < NCH - 1 else TI
                    emitter(lambda t, c=c, slh=slh, bankB=bankB, Qs=Qs: t.matmul(
                        psB[bankB][:, :], wsl12[slh][:, 1, c, :], xn[:, c, Qs],
                        start=(c == 0), stop=(c == NCH - 1)))
                tmm = ct.n
                w_use[("12", slh)] = tmm
                wt("s", ct, tmm)
                h_ev[("t1", ht % 2)] = SI(lambda s, ht=ht, bankA=bankA: s.activation(
                    t1f[ht % 2][:], psB[bankA][:, :], AF.Silu, scale=vcs("w1s", ht)))
                wt("v", cs, h_ev[("t1", ht % 2)])
                h_ev[("h", ht % 2)] = VI(lambda v, ht=ht, bankB=bankB: v.scalar_tensor_tensor(
                    hA[:, ht, :], psB[bankB][:, :], vcs("w2s", ht), t1f[ht % 2][:],
                    OP.mult, OP.mult))
            wt("t", cv, cv.n)
            for ot in range(32):
                slh = ot % NW
                wt("g", ct, w_use.get(("H", slh), 0))
                lw = wdma(slh)(qH[slh], wslH[slh][:], h1wt[ot, :, :])
                wt("t", qH[slh], lw)
                bank = 4 + ot % 2
                wt("t", cs, h_ev.get(("g1", ot % 2), 0))
                for it in range(32):
                    emitter = TN if it < 31 else TI
                    emitter(lambda t, it=it, slh=slh, bank=bank: t.matmul(
                        psB[bank][:, :], wslH[slh][:, it, :], hA[:, it, :],
                        start=(it == 0), stop=(it == 31)))
                tmm = ct.n
                w_use[("H", slh)] = tmm
                wt("s", ct, tmm)
                h_ev[("g1", ot % 2)] = SI(lambda s, ot=ot, bank=bank: s.activation(
                    hB[:, ot, :], psB[bank][:, :], AF.Gelu, scale=vcs("h1s", ot)))
            wt("t", cs, cs.n)
            for ot in range(32):
                slh = ot % NW
                wt("g", ct, w_use.get(("H", slh), 0))
                lw = wdma(slh)(qH[slh], wslH[slh][:], h2wt[ot, :, :])
                wt("t", qH[slh], lw)
                bank = 4 + ot % 2
                wt("t", cs, h_ev.get(("g2", ot % 2), 0))
                for it in range(32):
                    emitter = TN if it < 31 else TI
                    emitter(lambda t, it=it, slh=slh, bank=bank: t.matmul(
                        psB[bank][:, :], wslH[slh][:, it, :], hB[:, it, :],
                        start=(it == 0), stop=(it == 31)))
                tmm = ct.n
                w_use[("H", slh)] = tmm
                wt("s", ct, tmm)
                h_ev[("g2", ot % 2)] = SI(lambda s, ot=ot, bank=bank: s.activation(
                    hA[:, ot, :], psB[bank][:, :], AF.Gelu, scale=vcs("h2s", ot)))
            wt("t", cs, cs.n)
            for oc in range(NCH):
                slh = oc % NW
                wt("g", ct, w_use.get(("H", slh), 0))
                lw = wdma(slh)(stream(f"q3_{slh}"), wslH[slh][:], w3wt[oc, :, :])
                wt("t", stream(f"q3_{slh}"), lw)
                bank = 6 + oc % 2
                wt("t", cs, h_ev.get(("g3", oc % 2), 0))
                for it in range(32):
                    emitter = TN if it < 31 else TI
                    emitter(lambda t, it=it, slh=slh, bank=bank: t.matmul(
                        psB[bank][:, :], wslH[slh][:, it, :], hA[:, it, :],
                        start=(it == 0), stop=(it == 31)))
                tmm = ct.n
                w_use[("H", slh)] = tmm
                wt("s", ct, tmm)
                wt("s", cv, out_use.get(oc % 2, 0))
                sgt = SI(lambda s, oc=oc, slc=oc % 2, bank=bank: s.activation(
                    gt2[slc][:], psB[bank][:, :], AF.Silu, scale=vcs("w3s", oc)))
                h_ev[("g3", oc % 2)] = sgt
                slc = oc % 2
                wt("v", cs, sgt)
                wt("v", qout[slc], w_use.get(("st", slc), 0))
                VN(lambda v, oc=oc, slc=slc, Qs=Qs: v.tensor_tensor(
                    sg2[slc][:], gt2[slc][:], smix[:, oc, Qs], OP.mult))
                VD()
                wt("v", qx2, xq_tot)
                cvy = VI(lambda v, oc=oc, slc=slc: v.tensor_tensor(
                    sg2[slc][:], sg2[slc][:], xc2[:, oc, :], OP.add))
                out_use[slc] = cvy
                wt("y", cv, cvy)
                qout[slc].n += 16
                Q["y"].append(lambda e, o=yv[:, oc, Qs], i=sg2[slc][:], h=qout[slc].h:
                              e.dma_start(out=o, in_=i).then_inc(h, 16))
                w_use[("st", slc)] = qout[slc].n
        wt("y", qout[0], qout[0].n)
        wt("y", qout[1], qout[1].n)

    wt("y", ct, ct.n)
    wt("y", cs, cs.n)
    wt("y", cv, cv.n)
    wt("y", qst, qst.n)
    for st in dbg_list:
        wt("y", st, st.n)

    blk = es.enter_context(nc.Block())
    def runner(key):
        def body(eng):
            for f in Q[key]:
                f(eng)
        return body
    blk.gpsimd(runner("g"))
    blk.tensor(runner("t"))
    blk.scalar(runner("s"))
    blk.vector(runner("v"))
    blk.sync(runner("y"))
    es.close()
    return nc


def _cc(a):
    return np.ascontiguousarray(a, dtype=np.float32)


def _prep(I):
    o = {}
    o["W1"] = _cc(I["L0_attn_W1"])
    o["W2"] = _cc(I["L0_attn_W2"])
    o["w1lT"] = _cc(I["L0_ffn_w1_l"].T)
    o["w2lT"] = _cc(I["L0_ffn_w2_l"].T)
    o["w1hT"] = _cc(I["L0_ffn_w1_h"].T)
    o["w2hT"] = _cc(I["L0_ffn_w2_h"].T)
    o["WqT"] = _cc(I["L1_attn_Wq"].T)
    o["WkT"] = _cc(I["L1_attn_Wk"].T)
    o["WvT"] = _cc(I["L1_attn_Wv"].T)
    o["Wqo"] = _cc(I["L1_attn_Wq"])
    o["Wko"] = _cc(I["L1_attn_Wk"])
    o["Wvo"] = _cc(I["L1_attn_Wv"])
    def tile_io(wT, nin, nout):
        a = np.asarray(wT, np.float32).reshape(nin // 128, 128, nout // 128, 128)
        return _cc(a.transpose(2, 1, 0, 3).reshape(nout // 128, 128, nin))
    o["w1wt"] = tile_io(I["L1_ffn_w1_w"].T, D, 4096)
    o["w2wt"] = tile_io(I["L1_ffn_w2_w"].T, D, 4096)
    o["h1wt"] = tile_io(I["L1_ffn_h1_w"].T, 4096, 4096)
    o["h2wt"] = tile_io(I["L1_ffn_h2_w"].T, 4096, 4096)
    o["w3wt"] = tile_io(I["L1_ffn_w3_w"].T, 4096, D)
    o["w1ht"] = tile_io(I["L0_ffn_w1_h"].T, D, 4096)
    o["w1lt"] = tile_io(I["L0_ffn_w1_l"].T, D, 512)
    o["w2lt"] = tile_io(I["L0_ffn_w2_l"].T, 512, D)
    W1r = np.asarray(I["L0_attn_W1"], np.float32).reshape(NCH, 128, 32, 128)
    o["W1p"] = _cc(np.stack([
        W1r[:, :, [p, 8 + p, 16 + p, 24 + p], :]
        .transpose(1, 0, 2, 3).reshape(128, NCH, 512)
        for p in range(8)]))

    vcol = np.zeros((128, 288), np.float32)
    off = 0
    def put(v, k):
        nonlocal off
        v = np.asarray(v, np.float32)
        assert v.size == 128 * k, (v.size, k)
        vcol[:, off:off + k] = v.reshape(k, 128).T
        off += k
    put(I["L0_attn_b1"], 32)
    put(I["L0_attn_b2"], 8)
    put(I["L0_attn_gamma"], 8)
    put(I["L0_attn_beta"], 8)
    put(I["L0_ffn_routing_token"], 8)
    put(I["L0_ffn_gamma_l"], 8)
    put(I["L0_ffn_gamma_h"], 8)
    put(I["L0_ffn_b1_l"], 4)
    put(I["L0_ffn_b2_l"], 8)
    put(I["L0_ffn_b1_h"], 32)
    put(I["L1_attn_bq"], 8)
    put(I["L1_attn_bk"], 8)
    put(I["L1_attn_bv"], 8)
    put(I["L1_ffn_w1_s"], 32)
    put(I["L1_ffn_w2_s"], 32)
    put(I["L1_ffn_h1_s"], 32)
    put(I["L1_ffn_h2_s"], 32)
    put(I["L1_ffn_w3_s"], 8)
    o["VCOL"] = vcol

    vrow = np.zeros((1, 8192), np.float32)
    off = 0
    def putr(v, k):
        nonlocal off
        vrow[0, off:off + k] = np.asarray(v, np.float32).ravel()
        off += k
    putr(I["L0_attn_b1"], 4096)
    putr(I["L0_ffn_b2_h"], 1024)
    putr(I["L1_attn_bv"], 1024)
    putr(I["L0_attn_b2"], 1024)
    o["VROW"] = vrow

    cst = np.zeros((128, 260), np.float32)
    cst[:, 0:128] = np.eye(128, dtype=np.float32)
    ii = np.arange(128)
    cst[:, 128:256] = (ii[:, None] <= ii[None, :]).astype(np.float32)
    cst[0:64, 256] = 1.0
    cst[64:128, 257] = 1.0
    cst[:, 258] = 1.0
    o["CONST"] = cst
    return o


def kernel(**inputs):
    shared = _prep(inputs)
    x = np.asarray(inputs["L0_attn_x"], np.float32)
    nc = build_program()
    in_maps = []
    for c in range(8):
        m = dict(shared)
        m["xT"] = _cc(x[c].T)
        in_maps.append(m)
    res = run_bass_kernel_spmd(nc, in_maps, core_ids=list(range(8)))
    out = np.stack([np.asarray(r["yT"]).T for r in res.results])
    return out.astype(np.float32)


# version 26 (final)
# speedup vs baseline: 1.0576x; 1.0576x over previous
import numpy as np
import ml_dtypes
from contextlib import ExitStack

import concourse.bass as bass
import concourse.mybir as mybir
from concourse.bass_utils import run_bass_kernel_spmd

F32 = mybir.dt.float32
BF16 = mybir.dt.bfloat16
I32 = mybir.dt.int32
AF = mybir.ActivationFunctionType
AX = mybir.AxisListType
OP = mybir.AluOpType

T = 2048
D = 1024
NCH = 8
NTT = 16
NTG = 4
KEFF = 1152.0
NHEAVY = 1024.0


class Ctr:
    def __init__(self, nc, es, name):
        self.h = es.enter_context(nc.semaphore(name))
        self.n = 0


def build_program(dbg=(), phase_limit=99):
    nc = bass.Bass("TRN2", target_bir_lowering=False, debug=False)
    dbg = set(dbg)

    dtn = lambda n, s, d, k: nc.dram_tensor(n, s, d, kind=k).ap()
    xT = dtn("xT", [D, T], F32, "ExternalInput")
    W1 = dtn("W1", [D, 4 * D], F32, "ExternalInput")
    W2d = dtn("W2", [D, D], F32, "ExternalInput")
    w1lT = dtn("w1lT", [D, 512], F32, "ExternalInput")
    w2lT = dtn("w2lT", [512, D], F32, "ExternalInput")
    w1hT = dtn("w1hT", [D, 4096], F32, "ExternalInput")
    w2hT = dtn("w2hT", [4096, D], F32, "ExternalInput")
    WT = {"q": dtn("WqT", [D, D], F32, "ExternalInput"),
          "k": dtn("WkT", [D, D], F32, "ExternalInput"),
          "v": dtn("WvT", [D, D], F32, "ExternalInput")}
    WO = {"q": dtn("Wqo", [D, D], F32, "ExternalInput"),
          "k": dtn("Wko", [D, D], F32, "ExternalInput"),
          "v": dtn("Wvo", [D, D], F32, "ExternalInput")}
    w1wt = dtn("w1wt", [32, 128, 1024], F32, "ExternalInput")
    w2wt = dtn("w2wt", [32, 128, 1024], F32, "ExternalInput")
    h1wt = dtn("h1wt", [32, 128, 4096], F32, "ExternalInput")
    h2wt = dtn("h2wt", [32, 128, 4096], F32, "ExternalInput")
    w3wt = dtn("w3wt", [8, 128, 4096], F32, "ExternalInput")
    w1ht = dtn("w1ht", [32, 128, 1024], F32, "ExternalInput")
    W1p = dtn("W1p", [8, 128, 8, 512], F32, "ExternalInput")
    w1lt = dtn("w1lt", [4, 128, 1024], F32, "ExternalInput")
    w2lt = dtn("w2lt", [8, 128, 512], F32, "ExternalInput")
    VCOL = dtn("VCOL", [128, 288], F32, "ExternalInput")
    VROW = dtn("VROW", [1, 8192], F32, "ExternalInput")
    CONST = dtn("CONST", [128, 260], F32, "ExternalInput")
    yTo = dtn("yT", [D, T], F32, "ExternalOutput")
    xsp = dtn("xsp", [D, T], F32, "Internal")

    es = ExitStack()

    class PView:
        def __init__(self, t, i):
            self.t, self.i = t, i
        def __getitem__(self, idx):
            if isinstance(idx, tuple):
                rs, cs = idx
            else:
                rs, cs = idx, slice(None)
            return self.t[rs, self.i, cs]

    _cms = {}
    def sbt(n, s, d):
        cm = nc.sbuf_tensor(n, s, d)
        t = cm.__enter__()
        _cms[n] = cm
        return t
    def sfree(*names):
        for n in names:
            _cms.pop(n).__exit__(None, None, None)
    psALL = es.enter_context(nc.psum_tensor("psA", [128, 4, 512], F32))
    psB = [PView(psALL, i) for i in range(4)] + \
          [es.enter_context(nc.psum_tensor(f"pb{i}", [128, 512], F32)) for i in range(4, 8)]

    vcol = sbt("vcol", [128, 288], F32)
    cst = sbt("cst", [128, 260], F32)
    tri_bf = sbt("tri_bf", [128, 128], BF16)
    ones_bf = sbt("ones_bf", [128, 128], BF16)
    onesr_bf = sbt("onesr_bf", [1, 128], BF16)
    onesr_f = sbt("onesr_f", [1, 128], F32)
    ones2h_bf = sbt("ones2h_bf", [128, 2], BF16)
    onesc_bf = sbt("onesc_bf", [128, 1], BF16)
    col = {nm: sbt("col_" + nm, [128, 16], F32) for nm in
           ("s", "E", "R", "junk", "part", "sat", "pref", "sel",
            "mua", "iva", "tmpa", "tmpb")}
    satc_bf = sbt("satc_bf", [128, 16], BF16)
    tiny = sbt("tiny", [1, 64], F32)
    xn = sbt("xn", [128, NCH, T], BF16)
    smix = sbt("smix", [128, NCH, T], BF16)
    slab = sbt("slab", [1, T], F32)
    slab2 = sbt("slab2", [1, T], F32)
    bcA = sbt("bcA", [128, T], F32)
    bcB = sbt("bcB", [128, T], F32)
    _ci = {k: i for i, k in enumerate(
        ("rcell", "rsum", "lo", "hi", "mid", "cnt", "npred", "w0", "w1", "w2"))}
    cell = lambda nm: tiny[0:1, _ci[nm]:_ci[nm] + 1]

    eye = cst[:, 0:128]
    tri_f = cst[:, 128:256]
    ones2h_f = cst[:, 256:258]
    onesc_f = cst[:, 258:259]

    qst = Ctr(nc, es, "qst")
    _streams = {}
    def stream(name):
        if name not in _streams:
            _streams[name] = Ctr(nc, es, name)
        return _streams[name]
    ct = Ctr(nc, es, "ct")
    cs = Ctr(nc, es, "cs")
    cv = Ctr(nc, es, "cv")

    Q = {"t": [], "s": [], "v": [], "g": [], "y": []}

    def y_dma(st, out, in_):
        st.n += 16
        Q["y"].append(lambda e, o=out, i=in_, h=st.h: e.dma_start(out=o, in_=i).then_inc(h, 16))
        return st.n

    def g_dma(st, out, in_):
        st.n += 16
        Q["g"].append(lambda e, o=out, i=in_, h=st.h: e.dma_start(out=o, in_=i).then_inc(h, 16))
        return st.n

    def y_store(out, in_):
        qst.n += 16
        Q["y"].append(lambda e, o=out, i=in_: e.dma_start(out=o, in_=i).then_inc(qst.h, 16))
        return qst.n

    def wt(q, ctr, v):
        if v > 0:
            Q[q].append(lambda e, h=ctr.h, vv=v: e.wait_ge(h, vv))

    def TN(emit): Q["t"].append(emit)
    def SN(emit): Q["s"].append(emit)
    def VN(emit): Q["v"].append(emit)

    def TI(emit):
        ct.n += 1
        Q["t"].append(lambda e, f=emit: f(e).then_inc(ct.h, 1))
        return ct.n

    def SI(emit):
        cs.n += 1
        Q["s"].append(lambda e, f=emit: f(e).then_inc(cs.h, 1))
        return cs.n

    def VI(emit):
        cv.n += 1
        Q["v"].append(lambda e, f=emit: f(e).then_inc(cv.h, 1))
        return cv.n

    VD = lambda: VN(lambda v: v.drain())

    dbg_list = []
    def dbg_add(name, src, shape, dtype=F32):
        if name in dbg:
            d = nc.dram_tensor("dbg_" + name, list(shape), dtype, kind="ExternalOutput").ap()
            wt("y", ct, ct.n)
            wt("y", cs, cs.n)
            wt("y", cv, cv.n)
            st = stream("qdbg_" + name)
            st.n += 16
            Q["y"].append(lambda e, o=d, i=src, h=st.h: e.dma_start(out=o, in_=i).then_inc(h, 16))
            dbg_list.append(st)
            for qq in ("t", "s", "v", "g"):
                wt(qq, st, 16)

    VC = {}
    _o = 0
    for nm, k in (("b1", 32), ("b2", 8), ("gm", 8), ("bt", 8), ("rt", 8),
                  ("gl", 8), ("gh", 8), ("b1l", 4), ("b2l", 8), ("b1h", 32),
                  ("bq", 8), ("bk", 8), ("bv", 8), ("w1s", 32), ("w2s", 32),
                  ("h1s", 32), ("h2s", 32), ("w3s", 8)):
        VC[nm] = _o; _o += k
    vcs = lambda nm, j: vcol[:, VC[nm] + j:VC[nm] + j + 1]
    VR = {}
    _o = 0
    for nm, k in (("b1", 4096), ("b2h", 1024), ("bv", 1024), ("b2", 1024)):
        VR[nm] = _o; _o += k
    vrs = lambda nm, a, b: VROW[0:1, VR[nm] + a:VR[nm] + b]

    qc = stream("qc")
    y_dma(qc, vcol[:], VCOL)
    ld_cst = y_dma(qc, cst[:], CONST)
    wt("v", qc, ld_cst)
    VI(lambda v: v.tensor_copy(tri_bf[:], tri_f))
    VI(lambda v: v.memset(ones_bf[:], 1.0))
    VI(lambda v: v.memset(onesr_bf[:], 1.0))
    VI(lambda v: v.memset(onesr_f[:], 1.0))
    VI(lambda v: v.memset(onesc_bf[:], 1.0))
    cv_cst = VI(lambda v: v.tensor_copy(ones2h_bf[:], ones2h_f))
    wt("t", cv, cv_cst)
    wt("s", cv, cv_cst)
    wt("g", qc, ld_cst)

    sqrot = [sbt(f"sqrot{i}", [128, T], F32) for i in range(2)]
    sel_bf = sbt("sel_bf", [1, T], BF16)
    selb = sbt("selb", [128, T], BF16)
    b2h_bf = sbt("b2h_bf", [1, D], BF16)
    xrot = [sbt(f"xrot{i}", [128, T], F32) for i in range(4)]
    xfrot = sqrot

    def emit_ln(x_src, with_s, stats_ready=False):
        xv = x_src.rearrange("(c p) t -> p c t", p=128)
        tdone = {}
        qx = [stream(f"qx{i}") for i in range(4)]
        for c in range(0 if stats_ready else NCH):
            sl = c % 4
            sq_sl = c % 2
            if c >= 4:
                wt("y", ct, tdone[c - 4])
            ldx = y_dma(qx[sl], xrot[sl][:], xv[:, c, :])
            if c >= 2:
                wt("v", ct, tdone[c - 2])
            wt("v", qx[sl], ldx)
            cvsq = VI(lambda v, a=sqrot[sq_sl], b=xrot[sl]: v.tensor_tensor(
                a[:], b[:], b[:], OP.mult))
            wt("t", qx[sl], ldx)
            wt("t", cv, cvsq)
            for g in range(NTG):
                TN(lambda t, b=xrot[sl], g=g, c=c: t.matmul(
                    psB[g][0:1, 0:512], onesc_f, b[:, g * 512:(g + 1) * 512],
                    start=(c == 0), stop=(c == NCH - 1)))
                emitter = TN if g < NTG - 1 else TI
                emitter(lambda t, a=sqrot[sq_sl], g=g, c=c: t.matmul(
                    psB[4 + g][0:1, 0:512], onesc_f, a[:, g * 512:(g + 1) * 512],
                    start=(c == 0), stop=(c == NCH - 1)))
            tdone[c] = ct.n
        if not stats_ready:
            wt("s", ct, tdone[NCH - 1])
            for g in range(NTG):
                SN(lambda s, g=g: s.copy(slab[0:1, g * 512:(g + 1) * 512], psB[g][0:1, 0:512]))
                emitter = SN if g < NTG - 1 else SI
                emitter(lambda s, g=g: s.copy(slab2[0:1, g * 512:(g + 1) * 512],
                                              psB[4 + g][0:1, 0:512]))
        sev = cs.n
        wt("v", cs, sev)
        VN(lambda v: v.tensor_scalar(slab[:], slab[:], 1.0 / D, None, OP.mult))
        VD()
        VN(lambda v: v.tensor_scalar(slab2[:], slab2[:], 1.0 / D, None, OP.mult))
        VN(lambda v: v.tensor_tensor(sqrot[0][0:1, 0:T], slab[:], slab[:], OP.mult))
        VD()
        VN(lambda v: v.tensor_tensor(slab2[:], slab2[:], sqrot[0][0:1, 0:T], OP.subtract))
        VD()
        cvv = VI(lambda v: v.tensor_scalar(slab2[:], slab2[:], 1e-5, None, OP.add))
        wt("s", cv, cvv)
        css = SI(lambda s: s.activation(slab2[:], slab2[:], AF.Sqrt))
        wt("v", cs, css)
        cvi = VI(lambda v: v.reciprocal(slab2[:], slab2[:]))
        wt("t", cv, cvi)
        for g in range(NTG):
            tb = TI(lambda t, g=g: t.matmul(psB[0][:, :], onesr_f[:],
                                            slab[0:1, g * 512:(g + 1) * 512], start=True, stop=True))
            wt("s", ct, tb)
            sb1 = SI(lambda s, g=g: s.copy(bcA[:, g * 512:(g + 1) * 512], psB[0][:, :]))
            wt("t", cs, sb1)
            tb2 = TI(lambda t, g=g: t.matmul(psB[1][:, :], onesr_f[:],
                                             slab2[0:1, g * 512:(g + 1) * 512], start=True, stop=True))
            wt("s", ct, tb2)
            sb2 = SI(lambda s, g=g: s.copy(bcB[:, g * 512:(g + 1) * 512], psB[1][:, :]))
            wt("t", cs, sb2)
        wt("v", cs, cs.n)
        tdone2 = {}
        for c in range(NCH):
            sl = c % 4
            sq_sl = c % 2
            if c >= 4:
                _, gctr, gval = tdone2[c - 4]
                wt("y", gctr, gval)
            elif not stats_ready:
                wt("y", ct, tdone[NCH - 4 + c])
            ldx = y_dma(qx[sl], xrot[sl][:], xv[:, c, :])
            wt("v", qx[sl], ldx)
            if c >= 2:
                _, gctr, gval = tdone2[c - 2]
                wt("v", gctr, gval)
            VN(lambda v, a=xfrot[sq_sl], b=xrot[sl]: v.tensor_tensor(a[:], b[:], bcA[:], OP.subtract))
            VD()
            if with_s:
                VN(lambda v, a=xfrot[sq_sl]: v.tensor_tensor(a[:], a[:], bcB[:], OP.mult))
                VD()
                cvx = VI(lambda v, a=xfrot[sq_sl], c=c: v.tensor_copy(xn[:, c, :], a[:]))
            else:
                cvx = VI(lambda v, a=xfrot[sq_sl], c=c: v.tensor_tensor(
                    xn[:, c, :], a[:], bcB[:], OP.mult))
            if with_s:
                wt("t", cv, cvx)
                for tt in range(NTT):
                    emitter = TN if tt < NTT - 1 else TI
                    emitter(lambda t, a=xfrot[sq_sl], c=c, tt=tt: t.matmul(
                        psB[6][:, tt:tt + 1], a[:, tt * 128:(tt + 1) * 128],
                        vcs("rt", c), start=(c == 0 and tt == 0),
                        stop=(c == NCH - 1 and tt == NTT - 1)))
                tdone2[c] = ("t", ct, ct.n)
            else:
                tdone2[c] = ("v", cv, cvx)
        if with_s:
            wt("s", ct, tdone2[NCH - 1][2])
            return SI(lambda s: s.copy(col["s"][:], psB[6][:, 0:16]))
        return None

    cs_scol = emit_ln(xT, with_s=True)
    dbg_add("xn0", xn[:, :, :], (128, NCH * T), BF16)
    dbg_add("s_col", col["s"][:], (128, 16))

    def routing_gen():
        wt("v", cs, cs_scol)
        cve = VI(lambda v: v.tensor_scalar(col["E"][:], col["s"][:], 44.0, None, OP.min))
        wt("s", cv, cve)
        csE = SI(lambda s: s.activation(col["E"][:], col["E"][:], AF.Exp))
        wt("v", cs, csE)
        VI(lambda v: v.memset(col["R"][:, 0:1], float(np.float32(2048.0 / KEFF))))
        VN(lambda v: v.tensor_scalar(col["junk"][:], col["s"][:], -1.0, None, OP.mult))
        VD()
        VN(lambda v: v.reduce_max(col["part"][:, 0:1], col["junk"][:], axis=AX.X))
        cvt0 = VI(lambda v: v.reduce_max(col["tmpa"][:, 0:1], col["s"][:], axis=AX.X))
        wt("t", cv, cvt0)
        tr1 = TI(lambda t: t.transpose(psB[4][0:1, 0:128], col["part"][:, 0:1], eye))
        tr2 = TI(lambda t: t.transpose(psB[5][0:1, 0:128], col["tmpa"][:, 0:1], eye))
        wt("v", ct, tr2)
        VN(lambda v: v.reduce_max(cell("lo"), psB[4][0:1, 0:128], axis=AX.X))
        VD()
        VN(lambda v: v.tensor_scalar(cell("lo"), cell("lo"), -1.0, None, OP.mult))
        VN(lambda v: v.reduce_max(cell("hi"), psB[5][0:1, 0:128], axis=AX.X))
        VD()
        cvhi = VI(lambda v: v.tensor_scalar(cell("hi"), cell("hi"), 1.0, None, OP.add))
        yield
        prevRc = 0
        for it in range(49):
            wt("v", cv, prevRc)
            VN(lambda v: v.tensor_scalar(col["junk"][:], col["E"][:], col["R"][:, 0:1],
                                         0.0, OP.min, OP.add))
            VD()
            cva = VI(lambda v: v.reduce_sum(col["part"][:, 1:2], col["junk"][:], axis=AX.X))
            wt("t", cv, cva)
            ta = TI(lambda t: t.matmul(psB[4][0:1, 0:1], col["part"][:, 1:2], onesc_f,
                                       start=True, stop=True))
            wt("v", ct, ta)
            cvr2 = VI(lambda v: v.tensor_scalar(cell("rcell"), psB[4][0:1, 0:1], 1.0 / KEFF,
                                                None, OP.mult))
            VD()
            wt("t", cv, cvr2)
            tb = TI(lambda t: t.matmul(psB[4][:, 1:2], onesr_f[:], cell("rcell"),
                                       start=True, stop=True))
            wt("v", ct, tb)
            prevRc = VI(lambda v: v.tensor_copy(col["R"][:, 0:1], psB[4][:, 1:2]))
            VD()
            if it % 2 == 1:
                yield
        dbg_add("Rfin", col["R"][:, 0:1], (128, 1))
        wt("v", cv, cvhi)
        for it in range(25):
            VN(lambda v: v.tensor_tensor(cell("mid"), cell("lo"), cell("hi"), OP.add))
            VD()
            cvm = VI(lambda v: v.tensor_scalar(cell("mid"), cell("mid"), 0.5, None, OP.mult))
            wt("t", cv, cvm)
            tm = TI(lambda t: t.matmul(psB[5][:, 1:2], onesr_f[:], cell("mid"),
                                       start=True, stop=True))
            wt("v", ct, tm)
            cvmc = VI(lambda v: v.tensor_copy(col["tmpa"][:, 2:3], psB[5][:, 1:2]))
            VD()
            wt("v", cv, cvmc)
            VN(lambda v: v.tensor_scalar(col["junk"][:], col["s"][:], col["tmpa"][:, 2:3],
                                         0.0, OP.is_ge, OP.add))
            VD()
            cvc = VI(lambda v: v.reduce_sum(col["part"][:, 3:4], col["junk"][:], axis=AX.X))
            wt("t", cv, cvc)
            tc = TI(lambda t: t.matmul(psB[5][0:1, 0:1], col["part"][:, 3:4], onesc_f,
                                       start=True, stop=True))
            wt("v", ct, tc)
            VN(lambda v: v.tensor_scalar(cell("cnt"), psB[5][0:1, 0:1], NHEAVY, None, OP.is_ge))
            VD()
            VN(lambda v: v.tensor_scalar(cell("npred"), cell("cnt"), -1.0, 1.0, OP.mult, OP.add))
            VN(lambda v: v.tensor_tensor(cell("w0"), cell("mid"), cell("lo"), OP.subtract))
            VN(lambda v: v.tensor_tensor(cell("w1"), cell("mid"), cell("hi"), OP.subtract))
            VD()
            VN(lambda v: v.tensor_tensor(cell("w0"), cell("w0"), cell("cnt"), OP.mult))
            VN(lambda v: v.tensor_tensor(cell("w1"), cell("w1"), cell("npred"), OP.mult))
            VD()
            VN(lambda v: v.tensor_tensor(cell("lo"), cell("lo"), cell("w0"), OP.add))
            cv_it = VI(lambda v: v.tensor_tensor(cell("hi"), cell("hi"), cell("w1"), OP.add))
            VD()
            if it % 2 == 1:
                yield
        cv_tau = cv_it
        wt("t", cv, cv_tau)
        ttb = TI(lambda t: t.matmul(psB[5][:, 1:2], onesr_f[:], cell("lo"), start=True, stop=True))
        wt("v", ct, ttb)
        cvtb2 = VI(lambda v: v.tensor_copy(col["tmpb"][:, 0:1], psB[5][:, 1:2]))
        VD()
        wt("v", cv, max(cvtb2, prevRc))
        VN(lambda v: v.tensor_scalar(col["sat"][:], col["E"][:], col["R"][:, 0:1],
                                     0.0, OP.is_ge, OP.add))
        VD()
        cvsat = VI(lambda v: v.tensor_copy(satc_bf[:], col["sat"][:]))
        yield
        wt("t", cv, cvsat)
        for tt in range(NTT):
            for c in range(tt + 1):
                emitter = TI if (tt == NTT - 1 and c == tt) else TN
                emitter(lambda t, tt=tt, c=c, m=(tri_bf if c == tt else ones_bf): t.matmul(
                    psB[5][:, 32 + tt:33 + tt], m[:], satc_bf[:, c:c + 1],
                    start=(tt == 0 and c == 0), stop=(tt == NTT - 1 and c == tt)))
        tpref = ct.n
        wt("s", ct, tpref)
        cspref = SI(lambda s: s.copy(col["pref"][:], psB[5][:, 32:48]))
        wt("v", cs, cspref)
        VN(lambda v: v.tensor_scalar(col["tmpa"][:], col["pref"][:], NHEAVY, None, OP.is_le))
        VN(lambda v: v.tensor_scalar(col["junk"][:], col["s"][:], col["tmpb"][:, 0:1],
                                     0.0, OP.is_ge, OP.add))
        VD()
        VN(lambda v: v.tensor_tensor(col["tmpa"][:], col["tmpa"][:], col["junk"][:], OP.subtract))
        VD()
        VN(lambda v: v.tensor_tensor(col["tmpa"][:], col["tmpa"][:], col["sat"][:], OP.mult))
        VD()
        cvsel = VI(lambda v: v.tensor_tensor(col["sel"][:], col["junk"][:], col["tmpa"][:], OP.add))
        dbg_add("sel_col", col["sel"][:], (128, 16))
        yield
        wt("t", cv, cvsel)
        for tt in range(NTT):
            trr = TI(lambda t, tt=tt: t.transpose(psB[4][0:1, 0:128], col["sel"][:, tt:tt + 1], eye))
            wt("s", ct, trr)
            srr = SI(lambda s, tt=tt: s.copy(sel_bf[0:1, tt * 128:(tt + 1) * 128],
                                             psB[4][0:1, 0:128]))
            wt("t", cs, srr)
            if tt % 6 == 5:
                yield
        for g in range(NTG):
            tsb = TI(lambda t, g=g: t.matmul(psB[4][:, :], onesr_bf[:],
                                             sel_bf[0:1, g * 512:(g + 1) * 512], start=True, stop=True))
            wt("s", ct, tsb)
            ssb = SI(lambda s, g=g: s.copy(selb[:, g * 512:(g + 1) * 512], psB[4][:, :]))
            wt("t", cs, ssb)
        dbg_add("sel_row", sel_bf[:], (1, T), BF16)

    rg = routing_gen()
    if phase_limit >= 2:
        next(rg, None)

    rms_t = {}
    proj_ev = {}
    v_ev = {}
    sc_ev = {}
    av_mm = {}
    au_ev = {}
    ghmask = {}
    ghmm = {}
    sg_use = {}
    spill_n = {}
    xng_v = {}
    gl_use = {}
    if phase_limit >= 3:
        xn_ready = cv.n
        for c in range(NCH):
            sl = c % 2
            if c >= 2:
                wt("v", ct, rms_t[c - 2])
            cvq = VI(lambda v, sl=sl, c=c: v.tensor_tensor(
                sqrot[sl][:], xn[:, c, :], xn[:, c, :], OP.mult))
            wt("t", cv, cvq)
            for g in range(NTG):
                emitter = TN if g < NTG - 1 else TI
                emitter(lambda t, sl=sl, g=g, c=c: t.matmul(
                    psB[g][0:1, 0:512], onesc_f, sqrot[sl][:, g * 512:(g + 1) * 512],
                    start=(c == 0), stop=(c == NCH - 1)))
            rms_t[c] = ct.n
        wt("s", ct, rms_t[NCH - 1])
        for g in range(NTG):
            emitter = SN if g < NTG - 1 else SI
            emitter(lambda s, g=g: s.copy(slab[0:1, g * 512:(g + 1) * 512], psB[g][0:1, 0:512]))
        s_rss = cs.n
        wt("s", cs, s_rss)
        css2 = SI(lambda s: s.activation(slab[:], slab[:], AF.Sqrt))
        wt("v", cs, css2)
        VN(lambda v: v.tensor_scalar(slab[:], slab[:], 1e-12, None, OP.max))
        VD()
        VN(lambda v: v.reciprocal(slab[:], slab[:]))
        VD()
        cvr = VI(lambda v: v.tensor_scalar(slab[:], slab[:], float(np.float32(np.sqrt(1024.0))), None, OP.mult))
        wt("t", cv, cvr)
        for g in range(NTG):
            tb = TI(lambda t, g=g: t.matmul(psB[4][:, :], onesr_f[:],
                                            slab[0:1, g * 512:(g + 1) * 512], start=True, stop=True))
            wt("s", ct, tb)
            sbx = SI(lambda s, g=g: s.copy(bcA[:, g * 512:(g + 1) * 512], psB[4][:, :]))
            wt("t", cs, sbx)
        s_invrb = cs.n
        dbg_add("invr_b", bcA[:, :], (128, T))

    if phase_limit >= 4:
        sfree("xrot3", "xrot2", "xrot1", "xrot0")
        m2_bf = sbt("m2_bf", [1, T], BF16)
        c1_bf = sbt("c1_bf", [1, D], BF16)
        au = sbt("au", [128, NCH, T], BF16)
        u_p = sbt("u_p", [128, T], BF16)
        q_p = sbt("q_p", [128, T], BF16)
        k_p = sbt("k_p", [128, T], BF16)
        v_p = sbt("v_p", [128, NTT, 128], BF16)
        pt = sbt("pt", [128, 2, 2048], BF16)
        b1v_bf = sbt("b1v_bf", [1, D], BF16)
        wpair = [sbt(f"wpair{i}", [128, NCH, 512], BF16) for i in range(2)]
        WOFF = {"u": 0, "v": 128, "q": 256, "k": 384}
        qwp = [stream("qwp0"), stream("qwp1")]

        ldb1v = g_dma(stream("qrow"), b1v_bf[:], vrs("b1", 1024, 2048))
        wt("t", stream("qrow"), ldb1v)
        wt("t", cv, xn_ready)

        pair_done_t = {}
        W1v = W1.rearrange("(c p) n -> p c n", p=128)
        for p in range(8):
            sl = p % 2
            if p >= 2:
                wt("g", ct, pair_done_t[p - 2])
            g_dma(qwp[sl], wpair[sl][:], W1p[p, :, :, :])
            ldw = qwp[sl].n
            wt("t", qwp[sl], ldw)
            if p > 0:
                wt("s", cv, cv.n)
                wt("t", cs, cs.n)
            for oi, (nm, dst, bj) in enumerate((("u", u_p, p), ("q", q_p, 16 + p),
                                                ("k", k_p, 24 + p))):
                for g in range(NTG):
                    bank = (oi * NTG + g) % 2
                    if p + oi + g > 0:
                        wt("t", cs, proj_ev.get(bank, 0))
                    for c in range(NCH):
                        emitter = TN if c < NCH - 1 else TI
                        emitter(lambda t, sl=sl, o=WOFF[nm], c=c, g=g, bank=bank: t.matmul(
                            psB[bank][:, :], wpair[sl][:, c, o:o + 128],
                            xn[:, c, g * 512:(g + 1) * 512],
                            start=(c == 0), stop=(c == NCH - 1)))
                    tmm = ct.n
                    wt("s", ct, tmm)
                    proj_ev[bank] = SI(lambda s, dst=dst, g=g, bj=bj, bank=bank: s.activation(
                        dst[:, g * 512:(g + 1) * 512], psB[bank][:, :], AF.Silu,
                        bias=vcs("b1", bj)))
                    if phase_limit >= 2 and g < 3:
                        next(rg, None)
            for tt in range(NTT):
                bank = 2 + tt % 2
                wt("t", cs, v_ev.get(bank, 0))
                for c in range(NCH):
                    TN(lambda t, sl=sl, c=c, tt=tt, bank=bank: t.matmul(
                        psB[bank][:, 0:128], xn[:, c, tt * 128:(tt + 1) * 128],
                        wpair[sl][:, c, 128:256], start=(c == 0), stop=False))
                tvm = TI(lambda t, sl=sl, tt=tt, bank=bank, p=p: t.matmul(
                    psB[bank][:, 0:128], onesr_bf[:], b1v_bf[0:1, p * 128:(p + 1) * 128],
                    start=False, stop=True))
                wt("s", ct, tvm)
                v_ev[bank] = SI(lambda s, tt=tt, bank=bank: s.activation(
                    v_p[:, tt, :], psB[bank][:, 0:128], AF.Silu))
            pair_done_t[p] = ct.n
            proj_all = cs.n
            wt("t", cs, proj_all)
            for g in range(NTG):
                for grp in range(8):
                    ptsl = grp % 2
                    t_half = {}
                    for tkk in range(2):
                        tk = 2 * grp + tkk
                        wt("t", cs, sc_ev.get("A" if tkk == 0 else "B", 0))
                        for h in range(2):
                            bank = 2 * tkk + h
                            emitter = TI if h == 1 else TN
                            emitter(lambda t, tk=tk, g=g, h=h, bank=bank: t.matmul(
                                psB[bank][:, :], k_p[64 * h:64 * h + 64, tk * 128:(tk + 1) * 128],
                                q_p[64 * h:64 * h + 64, g * 512:(g + 1) * 512],
                                start=True, stop=True, tile_position=(64 * h, 0)))
                        t_half[tkk] = ct.n
                    t_sc = ct.n
                    if grp > 0:
                        pgrp = grp - 1
                        if grp == 1:
                            wt("t", cv, au_ev.get(6, 0))
                        for tkk in range(2):
                            tk = 2 * pgrp + tkk
                            TN(lambda t, tk=tk, psl=1 - ptsl, tkk=tkk: t.matmul(
                                psB[6][0:64, :], v_p[:, tk, 0:64],
                                pt[:, psl, (2 * tkk) * 512:(2 * tkk + 1) * 512],
                                start=(tk == 0), stop=False, tile_position=(0, 0)))
                            emitter = TI if tkk == 1 else TN
                            emitter(lambda t, tk=tk, psl=1 - ptsl, tkk=tkk: t.matmul(
                                psB[7][64:128, :], v_p[:, tk, 64:128],
                                pt[:, psl, (2 * tkk + 1) * 512:(2 * tkk + 2) * 512],
                                start=(tk == 0), stop=False,
                                tile_position=(0, 64), skip_group_check=True))
                        av_mm[1 - ptsl] = ct.n
                        if grp == 2 and phase_limit >= 2:
                            next(rg, None)
                    wt("s", ct, t_half[0])
                    wt("s", ct, av_mm.get(ptsl, 0))
                    sc_ev["A"] = SI(lambda s, ptsl=ptsl: s.activation(
                        pt[:, ptsl, 0:1024], psALL[:, 0:2, :], AF.Silu))
                    wt("s", ct, t_sc)
                    sc_ev["B"] = SI(lambda s, ptsl=ptsl: s.activation(
                        pt[:, ptsl, 1024:2048], psALL[:, 2:4, :], AF.Silu))
                ptsl = 7 % 2
                wt("t", cs, sc_ev["B"])
                for tkk in range(2):
                    tk = 14 + tkk
                    TN(lambda t, tk=tk, ptsl=ptsl, tkk=tkk: t.matmul(
                        psB[6][0:64, :], v_p[:, tk, 0:64],
                        pt[:, ptsl, (2 * tkk) * 512:(2 * tkk + 1) * 512],
                        start=False, stop=(tkk == 1), tile_position=(0, 0)))
                    emitter = TI if tkk == 1 else TN
                    emitter(lambda t, tk=tk, ptsl=ptsl, tkk=tkk: t.matmul(
                        psB[7][64:128, :], v_p[:, tk, 64:128],
                        pt[:, ptsl, (2 * tkk + 1) * 512:(2 * tkk + 2) * 512],
                        start=False, stop=(tkk == 1),
                        tile_position=(0, 64), skip_group_check=True))
                av_mm[ptsl] = ct.n
                wt("v", ct, av_mm[ptsl])
                VN(lambda v, p=p, g=g: v.scalar_tensor_tensor(
                    au[0:64, p, g * 512:(g + 1) * 512], psB[6][0:64, :], 1.0 / 2048.0,
                    u_p[0:64, g * 512:(g + 1) * 512], OP.mult, OP.mult))
                au_ev[6] = VI(lambda v, p=p, g=g: v.scalar_tensor_tensor(
                    au[64:128, p, g * 512:(g + 1) * 512], psB[7][64:128, :], 1.0 / 2048.0,
                    u_p[64:128, g * 512:(g + 1) * 512], OP.mult, OP.mult))

        dbg_add("au", au[:, :, :], (128, NCH * T), BF16)

    if phase_limit >= 5:
        if phase_limit >= 2:
            for _ in rg:
                pass
        sfree("wpair1", "wpair0", "b1v_bf", "pt", "v_p", "k_p", "q_p", "u_p")
        W2sb = sbt("W2sb", [128, NCH, D], BF16)
        beta_bf = sbt("beta_bf", [128, 8], BF16)
        c2b_col = sbt("c2b_col", [128, 8], F32)
        qw2 = stream("qw2")
        wt("g", ct, ct.n)
        wt("g", cv, cv.n)
        g_dma(qw2, W2sb[:], W2d.rearrange("(c p) n -> p c n", p=128))
        ldw2 = qw2.n
        cvbeta = VI(lambda v: v.tensor_copy(beta_bf[:], vcol[:, VC["bt"]:VC["bt"] + 8]))
        wt("t", qw2, ldw2)
        wt("t", cv, cv.n)
        wt("t", cs, cs.n)
        for h in range(2):
            for c in range(NCH):
                emitter = TN if c < NCH - 1 else TI
                emitter(lambda t, h=h, c=c: t.matmul(
                    psB[2 + h][0:1, 0:512], beta_bf[:, c:c + 1],
                    W2sb[:, c, h * 512:(h + 1) * 512], start=(c == 0), stop=(c == NCH - 1)))
        t_c2 = ct.n
        wt("s", ct, t_c2)
        SN(lambda s: s.copy(slab2[0:1, 0:512], psB[2][0:1, 0:512]))
        s_c2 = SI(lambda s: s.copy(slab2[0:1, 512:1024], psB[3][0:1, 0:512]))
        wt("v", ct, t_c2)
        for c in range(NCH):
            emitter = VN if c < NCH - 1 else VI
            emitter(lambda v, c=c: v.tensor_scalar(W2sb[:, c, :], W2sb[:, c, :],
                                                   vcs("gm", c), 0.0, OP.mult, OP.add))
        cv_gfold = cv.n
        wt("y", cv, cv.n)
        ldb2 = y_dma(stream("qrow2"), sqrot[1][0:1, 0:1024], vrs("b2", 0, 1024))
        wt("v", stream("qrow2"), ldb2)
        wt("v", cs, s_c2)
        cv_c2b = VI(lambda v: v.tensor_tensor(slab2[0:1, 0:1024], slab2[0:1, 0:1024],
                                              sqrot[1][0:1, 0:1024], OP.add))
        wt("t", cv, cv_c2b)
        for oc in range(NCH):
            ttr = TI(lambda t, oc=oc: t.transpose(psB[4][0:128, 0:1],
                                                  slab2[0:1, oc * 128:(oc + 1) * 128], eye[0:1, 0:1]))
            wt("s", ct, ttr)
            str_ = SI(lambda s, oc=oc: s.copy(c2b_col[:, oc:oc + 1], psB[4][0:128, 0:1]))
            wt("t", cs, str_)
        wt("t", cv, cv_gfold)
        wt("t", cs, s_c2)
        for h in range(2):
            for c in range(NCH):
                emitter = TN if c < NCH - 1 else TI
                emitter(lambda t, h=h, c=c: t.matmul(
                    psB[2 + h][0:1, 0:512], onesc_bf[:], W2sb[:, c, h * 512:(h + 1) * 512],
                    start=(c == 0), stop=(c == NCH - 1)))
        t_c1 = ct.n
        wt("s", ct, t_c1)
        SN(lambda s: s.copy(c1_bf[0:1, 0:512], psB[2][0:1, 0:512]))
        s_c1 = SI(lambda s: s.copy(c1_bf[0:1, 512:1024], psB[3][0:1, 0:512]))
        VD()
        wt("t", cs, s_c1)
        au_st = {}
        for c in range(NCH):
            sl = c % 2
            if c >= 2:
                wt("v", ct, au_st[c - 2])
            cvq2 = VI(lambda v, sl=sl, c=c: v.tensor_tensor(
                sqrot[sl][:], au[:, c, :], au[:, c, :], OP.mult))
            wt("t", cv, cvq2)
            for g in range(NTG):
                TN(lambda t, c=c, g=g: t.matmul(
                    psB[g][0:1, 0:512], onesc_bf[:], au[:, c, g * 512:(g + 1) * 512],
                    start=(c == 0), stop=(c == NCH - 1)))
            for g in range(NTG):
                emitter = TN if g < NTG - 1 else TI
                emitter(lambda t, sl=sl, c=c, g=g: t.matmul(
                    psB[4 + g][0:1, 0:512], onesc_f, sqrot[sl][:, g * 512:(g + 1) * 512],
                    start=(c == 0), stop=(c == NCH - 1)))
            au_st[c] = ct.n
        wt("s", ct, au_st[NCH - 1])
        for g in range(NTG):
            SN(lambda s, g=g: s.copy(slab[0:1, g * 512:(g + 1) * 512], psB[g][0:1, 0:512]))
            emitter = SN if g < NTG - 1 else SI
            emitter(lambda s, g=g: s.copy(slab2[0:1, g * 512:(g + 1) * 512], psB[4 + g][0:1, 0:512]))
        s_aust = cs.n
        wt("v", cs, s_aust)
        VN(lambda v: v.tensor_scalar(slab[:], slab[:], 1.0 / D, None, OP.mult))
        VD()
        VN(lambda v: v.tensor_scalar(slab2[:], slab2[:], 1.0 / D, None, OP.mult))
        VN(lambda v: v.tensor_tensor(sqrot[0][0:1, 0:T], slab[:], slab[:], OP.mult))
        VD()
        VN(lambda v: v.tensor_tensor(slab2[:], slab2[:], sqrot[0][0:1, 0:T], OP.subtract))
        VD()
        cvv2 = VI(lambda v: v.tensor_scalar(slab2[:], slab2[:], 1e-5, None, OP.add))
        wt("s", cv, cvv2)
        css3 = SI(lambda s: s.activation(slab2[:], slab2[:], AF.Sqrt))
        wt("v", cs, css3)
        VN(lambda v: v.reciprocal(slab2[:], slab2[:]))
        VN(lambda v: v.tensor_scalar(sqrot[1][0:1, 0:T], slab[:], -1.0, None, OP.mult))
        VD()
        cvm2 = VI(lambda v: v.tensor_copy(m2_bf[:], sqrot[1][0:1, 0:T]))
        wt("t", cv, cvm2)
        for g in range(NTG):
            tb = TI(lambda t, g=g: t.matmul(psB[4][:, :], onesr_f[:],
                                            slab2[0:1, g * 512:(g + 1) * 512], start=True, stop=True))
            wt("s", ct, tb)
            sbx = SI(lambda s, g=g: s.copy(bcB[:, g * 512:(g + 1) * 512], psB[4][:, :]))
            wt("t", cs, sbx)
        s_iaub = cs.n
        VD()
        wt("t", cs, max(s_c1, s_iaub))
        smix_ev = {}
        for g in range(NTG):
            for oc in range(NCH):
                bank = 2 + (g * NCH + oc) % 2
                wt("t", cv, smix_ev.get(bank, 0))
                for c in range(NCH):
                    TN(lambda t, c=c, g=g, oc=oc, bank=bank: t.matmul(
                        psB[bank][:, :], W2sb[:, c, oc * 128:(oc + 1) * 128],
                        au[:, c, g * 512:(g + 1) * 512], start=(c == 0), stop=False))
                tmm = TI(lambda t, g=g, oc=oc, bank=bank: t.matmul(
                    psB[bank][:, :], c1_bf[0:1, oc * 128:(oc + 1) * 128],
                    m2_bf[0:1, g * 512:(g + 1) * 512], start=False, stop=True))
                wt("v", ct, tmm)
                VN(lambda v, g=g, bank=bank: v.tensor_tensor(
                    sqrot[1][:, 0:512], psB[bank][:, :], bcB[:, g * 512:(g + 1) * 512], OP.mult))
                VD()
                smix_ev[bank] = VI(lambda v, g=g, oc=oc: v.tensor_scalar(
                    smix[:, oc, g * 512:(g + 1) * 512], sqrot[1][:, 0:512],
                    c2b_col[:, oc:oc + 1], 0.0, OP.add, OP.add))
                VD()
        dbg_add("smix0", smix[:, :, :], (128, NCH * T), BF16)

    if phase_limit >= 6:
        sfree("c2b_col", "beta_bf", "W2sb", "au", "c1_bf", "m2_bf")
        QT = 512
        NQ = T // QT
        w1l_b = [sbt(f"w1lb{i}", [128, NCH, 128], BF16) for i in range(2)]
        w2l_b = [sbt(f"w2lb{i}", [128, 4, 128], BF16) for i in range(2)]
        xng = sbt("xng", [128, NCH, QT], BF16)
        gl_q = sbt("gl_q", [128, 4, QT], BF16)
        ghr = [sbt(f"ghr{i}", [128, QT], BF16) for i in range(2)]
        ghbuf = sbt("ghbuf", [128, 32, QT], BF16)
        w1h_b = [sbt(f"w1hb{i}", [128, NCH, 128], BF16) for i in range(2)]
        w2h_b = [sbt(f"w2hb{i}", [128, 2, 1024], BF16) for i in range(2)]
        gt_b = [sbt(f"gtb{i}", [128, QT], BF16) for i in range(2)]
        sg_f = [sbt(f"sgf{i}", [128, QT], F32) for i in range(2)]
        sq6v = [bcB[:, 0:QT], bcB[:, QT:2 * QT]]
        xc_b = sbt("xcb", [128, 4, QT], F32)
        w1hS = dtn("w1hS", [32, 128, 1024], BF16, "Internal")
        w2hS = dtn("w2hS", [16, 128, 2048], BF16, "Internal")
        ql = [stream("ql0"), stream("ql1")]
        ql2 = [stream("ql20"), stream("ql21")]
        qh1 = [stream("qh10"), stream("qh11")]
        qh2 = [stream("qh20"), stream("qh21")]
        qstg = stream("qstg")
        qxc = [stream("qxc0"), stream("qxc1")]
        qsp = [stream("qsp0"), stream("qsp1")]

        def y_sdma(st, out, in_):
            st.n += 16
            Q["y"].append(lambda e, o=out, i=in_, h=st.h: e.dma_start(out=o, in_=i).then_inc(h, 16))
            return st.n

        w2hv = w2hT.rearrange("(c p) n -> p c n", p=128)
        xv0 = xT.rearrange("(c p) t -> p c t", p=128)
        xspv = xsp.rearrange("(c p) t -> p c t", p=128)
        ldb2h = g_dma(stream("qrow"), b2h_bf[:], vrs("b2h", 0, 1024))
        wt("g", ct, ct.n)
        wt("g", cv, cv.n)
        wt("v", ct, ct.n)
        wt("y", ct, ct.n)
        wt("y", cv, cv.n)
        wt("t", stream("qrow"), stream("qrow").n)
        wg = {}
        h1ev = {}
        stat_t = {}
        glq_ev = {}
        maskv = {}
        o_ev = {}
        xng_guard = 0
        stg_n = {}
        for Qi in range(NQ):
            Qs = slice(Qi * QT, (Qi + 1) * QT)
            wt("y", cv, o_ev.get("xcA", 0))
            for oc in range(4):
                y_dma(qxc[0], xc_b[:, oc, :], xv0[:, oc, Qs])
            xcA = qxc[0].n
            wt("v", ct, xng_guard)
            for c in range(NCH):
                emitter = VN if c < NCH - 1 else VI
                emitter(lambda v, c=c, Qs=Qs: v.scalar_tensor_tensor(
                    xng[:, c, :], xn[:, c, Qs], vcs("gl", c), bcA[:, Qs], OP.mult, OP.mult))
            cv_xngl = cv.n
            wt("t", cv, cv_xngl)
            for m in range(4):
                sl = m % 2
                bank = m % 2
                wt("g", ct, wg.get(("l", sl), 0))
                lw = g_dma(ql[sl], w1l_b[sl][:], w1lt[m, :, :])
                wt("t", ql[sl], lw)
                wt("t", cs, glq_ev.get(bank, 0))
                for c in range(NCH):
                    emitter = TN if c < NCH - 1 else TI
                    emitter(lambda t, m=m, c=c, sl=sl, bank=bank: t.matmul(
                        psB[bank][:, :], w1l_b[sl][:, c, :], xng[:, c, :],
                        start=(c == 0), stop=(c == NCH - 1)))
                tmm = ct.n
                wg[("l", sl)] = tmm
                wt("s", ct, tmm)
                glq_ev[bank] = SI(lambda s, m=m, bank=bank: s.activation(
                    gl_q[:, m, :], psB[bank][:, :], AF.Gelu, bias=vcs("b1l", m)))
            light_done = ct.n
            wt("v", ct, light_done)
            for c in range(NCH):
                emitter = VN if c < NCH - 1 else VI
                emitter(lambda v, c=c, Qs=Qs: v.scalar_tensor_tensor(
                    xng[:, c, :], xn[:, c, Qs], vcs("gh", c), bcA[:, Qs], OP.mult, OP.mult))
            cv_xngh = cv.n
            wt("t", cv, cv_xngh)
            for ht in range(32):
                sl = ht % 2
                bank = 2 + ht % 2
                if Qi == 0:
                    wt("g", ct, wg.get(("h1", sl), 0))
                    wt("g", stream(f"qstg1{sl}"), stg_n.get(("h1", sl), 0))
                    lw = g_dma(qh1[sl], w1h_b[sl][:], w1ht[ht, :, :])
                    wt("t", qh1[sl], lw)
                    wt("y", qh1[sl], lw)
                    stg_n[("h1", sl)] = y_sdma(stream(f"qstg1{sl}"), w1hS[ht, :, :],
                                               w1h_b[sl][:].rearrange("p c n -> p (c n)"))
                else:
                    wt("y", ct, wg.get(("h1", sl), 0))
                    wt("y", stream(f"qstg1{sl}"), stg_n.get(("h1", sl), 0))
                    lw = y_dma(stream(f"qh1s{sl}"), w1h_b[sl][:], w1hS[ht, :, :])
                    wt("t", stream(f"qh1s{sl}"), lw)
                wt("t", cs, h1ev.get(bank, 0))
                for c in range(NCH):
                    emitter = TN if c < NCH - 1 else TI
                    emitter(lambda t, ht=ht, c=c, sl=sl, bank=bank: t.matmul(
                        psB[bank][:, :], w1h_b[sl][:, c, :], xng[:, c, :],
                        start=(c == 0), stop=(c == NCH - 1)))
                tmm = ct.n
                wg[("h1", sl)] = tmm
                wt("s", ct, tmm)
                wt("s", cv, maskv.get(("slot", ht % 2), 0))
                sgh = SI(lambda s, ht=ht, sl=ht % 2, bank=bank: s.activation(
                    ghr[sl][:], psB[bank][:, :], AF.Gelu, bias=vcs("b1h", ht)))
                h1ev[bank] = sgh
                wt("v", cs, sgh)
                maskv[("slot", ht % 2)] = maskv[ht] = VI(
                    lambda v, ht=ht, sl=ht % 2, Qs=Qs: v.tensor_tensor(
                        ghbuf[:, ht, :], ghr[sl][:], selb[:, Qs], OP.mult))
            xng_guard = ct.n

            wt("t", cs, cs.n)
            for oc in range(NCH):
                sl = oc % 2
                wt("g", ct, wg.get(("l2", sl), 0))
                lw = g_dma(ql2[sl], w2l_b[sl][:], w2lt[oc, :, :])
                wt("t", ql2[sl], lw)
                if Qi > 0:
                    wt("t", cv, o_ev.get(oc, 0))
                for c2 in range(4):
                    emitter = TN if c2 < 3 else TI
                    emitter(lambda t, oc=oc, c2=c2, sl=sl: t.matmul(
                        psB[oc][:, :], w2l_b[sl][:, c2, :], gl_q[:, c2, :],
                        start=(c2 == 0), stop=False))
                wg[("l2", sl)] = ct.n
            for hb in range(16):
                sl = hb % 2
                if Qi == 0:
                    wt("g", ct, wg.get(("h2", sl), 0))
                    wt("g", stream(f"qstg2{sl}"), stg_n.get(("h2", sl), 0))
                    lw = g_dma(qh2[sl], w2h_b[sl][:], w2hv[:, hb * 2:(hb + 1) * 2, :])
                    wt("t", qh2[sl], lw)
                    wt("y", qh2[sl], lw)
                    stg_n[("h2", sl)] = y_sdma(stream(f"qstg2{sl}"), w2hS[hb, :, :],
                                               w2h_b[sl][:].rearrange("p c n -> p (c n)"))
                else:
                    wt("y", ct, wg.get(("h2", sl), 0))
                    wt("y", stream(f"qstg2{sl}"), stg_n.get(("h2", sl), 0))
                    lw = y_dma(stream(f"qh2s{sl}"), w2h_b[sl][:], w2hS[hb, :, :])
                    wt("t", stream(f"qh2s{sl}"), lw)
                for hq in range(2):
                    ht = hb * 2 + hq
                    wt("t", cv, maskv[ht])
                    for oc in range(NCH):
                        emitter = TN if oc < NCH - 1 else TI
                        emitter(lambda t, oc=oc, ht=ht, hq=hq, sl=sl: t.matmul(
                            psB[oc][:, :], w2h_b[sl][:, hq, oc * 128:(oc + 1) * 128],
                            ghbuf[:, ht, :], start=False, stop=False))
                wg[("h2", sl)] = ct.n
            for oc in range(NCH):
                emitter = TN if oc < NCH - 1 else TI
                emitter(lambda t, oc=oc, Qi=Qi: t.matmul(
                    psB[oc][:, :], b2h_bf[0:1, oc * 128:(oc + 1) * 128],
                    sel_bf[0:1, Qi * QT:(Qi + 1) * QT], start=False, stop=True))
            t_oend = ct.n
            xcB = [0]
            stat_mm = {}
            for oc in range(NCH):
                slc = oc % 2
                wt("s", ct, t_oend)
                wt("s", cv, o_ev.get(("gt", slc), 0))
                sgt = SI(lambda s, oc=oc, slc=slc: s.activation(
                    gt_b[slc][:], psB[oc][:, :], AF.Silu, bias=vcs("b2l", oc)))
                wt("v", cs, sgt)
                wt("v", qsp[slc], stg_n.get(("sp", slc), 0))
                wt("v", ct, stat_t.get(slc, 0))
                VN(lambda v, oc=oc, slc=slc, Qs=Qs: v.tensor_tensor(
                    sg_f[slc][:], gt_b[slc][:], smix[:, oc, Qs], OP.mult))
                VD()
                wt("v", qxc[0] if oc < 4 else qxc[1], xcA if oc < 4 else xcB[0])
                cvy = VI(lambda v, oc=oc, slc=slc: v.tensor_tensor(
                    sg_f[slc][:], sg_f[slc][:], xc_b[:, oc % 4, :], OP.add))
                VD()
                cvsq6 = VI(lambda v, oc=oc, slc=slc: v.tensor_tensor(
                    sq6v[slc], sg_f[slc][:], sg_f[slc][:], OP.mult))
                stat_mm[oc] = (cvy, cvsq6, slc)
                o_ev[("gt", slc)] = cvsq6
                o_ev[oc] = cvy
                if oc == 3:
                    o_ev["xcA"] = cvy
                    wt("y", cv, cvy)
                    for oc2 in range(4):
                        y_dma(qxc[1], xc_b[:, oc2, :], xv0[:, 4 + oc2, Qs])
                    xcB[0] = qxc[1].n
                if oc == 7:
                    o_ev["xcB"] = cvy
                wt("y", cv, cvy)
                stg_n[("sp", slc)] = y_sdma(qsp[slc], xspv[:, oc, Qs], sg_f[slc][:])
                if oc >= 1:
                    pcv, pcq, pslc = stat_mm[oc - 1]
                    wt("t", cv, pcq)
                    if oc == 1:
                        wt("t", cs, sgt)
                    TN(lambda t, pslc=pslc, oc=oc: t.matmul(
                        psB[0][0:1, 0:512], onesc_f, sg_f[pslc][:],
                        start=(oc == 1), stop=False))
                    tq6 = TI(lambda t, pslc=pslc, oc=oc: t.matmul(
                        psB[1][0:1, 0:512], onesc_f, sq6v[pslc],
                        start=(oc == 1), stop=False))
                    stat_t[pslc] = tq6
            pcv, pcq, pslc = stat_mm[7]
            wt("t", cv, pcq)
            TN(lambda t, pslc=pslc: t.matmul(psB[0][0:1, 0:512], onesc_f, sg_f[pslc][:],
                                             start=False, stop=True))
            tst = TI(lambda t, pslc=pslc: t.matmul(psB[1][0:1, 0:512], onesc_f,
                                                   sq6v[pslc], start=False, stop=True))
            stat_t[pslc] = tst
            wt("s", ct, tst)
            SN(lambda s, Qs=Qs: s.copy(slab[0:1, Qs], psB[0][0:1, 0:512]))
            s_st6 = SI(lambda s, Qs=Qs: s.copy(slab2[0:1, Qs], psB[1][0:1, 0:512]))
            o_ev[0] = max(o_ev[0], 0)
            wt("t", cs, s_st6)
        sp_total = qst.n

    if phase_limit >= 7:
        sfree("xcb", "sgf1", "sgf0", "gtb1", "gtb0",
              "w2hb1", "w2hb0", "w1hb1", "w1hb0", "ghbuf", "ghr1", "ghr0",
              "gl_q", "xng", "w2lb1", "w2lb0", "w1lb1", "w1lb0",
              "b2h_bf", "selb", "sel_bf")
        xrot = [sbt(f"x2rot{i}", [128, T], F32) for i in range(4)]
        wt("y", qsp[0], qsp[0].n)
        wt("y", qsp[1], qsp[1].n)
        wt("y", ct, ct.n)
        wt("y", cs, cs.n)
        wt("y", cv, cv.n)
        wt("t", cs, cs.n)
        wt("v", cs, cs.n)
        wt("v", qsp[0], qsp[0].n)
        wt("v", qsp[1], qsp[1].n)
        emit_ln(xsp, with_s=False, stats_ready=True)
        dbg_add("xn1", xn[:, :, :], (128, NCH * T), BF16)

    if phase_limit >= 8:
        sfree("x2rot3", "x2rot2", "x2rot1", "x2rot0")
        scC = {w: sbt(f"scC_{w}", [128, 8], F32) for w in ("q", "k", "v")}
        bvp_bf = sbt("bvp_bf", [1, D], BF16)
        wqi = {w: sbt(f"wqi_{w}", [128, NCH, D], BF16) for w in ("q", "k", "v")}
        iq_t = sbt("iq_t", [128, D], I32)
        qoi = [stream("qoi0"), stream("qoi1")]
        qio = [stream("qio0"), stream("qio1")]
        recip_b = bcA[:, 0:D]
        sc_b = bcA[:, D:2 * D]
        wt("y", ct, ct.n)
        wt("y", cv, cv.n)
        wt("g", ct, ct.n)
        wt("g", cv, cv.n)
        quant_v = {}
        WS = ("q", "k", "v")
        rb = {"q": bcA[:, 0:D], "k": bcB[:, 0:D], "v": bcA[:, 0:D]}

        def issue_am(w, c, guard):
            sl = c % 2
            if guard:
                wt("g", cv, guard)
            quant_v[(w, "am_ld", c)] = g_dma(
                qoi[sl], sqrot[0][:, sl * 1024:(sl + 1) * 1024],
                WO[w].rearrange("(c p) n -> p c n", p=128)[:, c, :])

        def emit_reduce(w, wi, c):
            sl = c % 2
            wt("v", qoi[sl], quant_v[(w, "am_ld", c)])
            r = VI(lambda v, w=w, c=c, sl=sl: v.tensor_reduce(
                scC[w][:, c:c + 1], sqrot[0][:, sl * 1024:(sl + 1) * 1024],
                axis=AX.X, op=OP.max, apply_absolute_value=True))
            VD()
            if c + 2 < NCH:
                issue_am(w, c + 2, r)
            elif wi < 2:
                issue_am(WS[wi + 1], c + 2 - NCH, r)

        def emit_dance(w):
            VN(lambda v, w=w: v.tensor_scalar(scC[w][:], scC[w][:], 1.0 / 127.0, 1e-8,
                                              OP.mult, OP.max))
            VD()
            cvrc = VI(lambda v, w=w: v.reciprocal(col["junk"][:, 0:8], scC[w][:]))
            wt("t", cv, cvrc)
            for oc in range(NCH):
                emitter = TN if oc < NCH - 1 else TI
                emitter(lambda t, oc=oc: t.transpose(
                    psB[4 + oc // 4][0:1, (oc % 4) * 128:(oc % 4) * 128 + 128],
                    col["junk"][:, oc:oc + 1], eye))
            ttr = ct.n
            wt("s", ct, ttr)
            SN(lambda s: s.copy(slab[0:1, 0:512], psB[4][0:1, 0:512]))
            s_tr = SI(lambda s: s.copy(slab[0:1, 512:1024], psB[5][0:1, 0:512]))
            wt("t", cs, s_tr)
            if w == "v":
                for oc in range(NCH):
                    emitter = TN if oc < NCH - 1 else TI
                    emitter(lambda t, oc=oc: t.transpose(
                        psB[4 + oc // 4][0:1, (oc % 4) * 128:(oc % 4) * 128 + 128],
                        scC["v"][:, oc:oc + 1], eye))
                ttr2 = ct.n
                wt("s", ct, ttr2)
                SN(lambda s: s.copy(slab2[0:1, 0:512], psB[4][0:1, 0:512]))
                s_tr2 = SI(lambda s: s.copy(slab2[0:1, 512:1024], psB[5][0:1, 0:512]))
                wt("t", cs, s_tr2)
            s_rows = cs.n
            wt("t", cs, s_rows)
            for h2 in range(2):
                tb = TI(lambda t, h2=h2: t.matmul(psB[6][:, :], onesr_f[:],
                                                  slab[0:1, h2 * 512:(h2 + 1) * 512],
                                                  start=True, stop=True))
                wt("s", ct, tb)
                sbx = SI(lambda s, w=w, h2=h2: s.copy(rb[w][:, h2 * 512:(h2 + 1) * 512],
                                                      psB[6][:, :]))
                wt("t", cs, sbx)
            if w == "v":
                for h2 in range(2):
                    tb = TI(lambda t, h2=h2: t.matmul(psB[7][:, :], onesr_f[:],
                                                      slab2[0:1, h2 * 512:(h2 + 1) * 512],
                                                      start=True, stop=True))
                    wt("s", ct, tb)
                    sbx = SI(lambda s, h2=h2: s.copy(sc_b[:, h2 * 512:(h2 + 1) * 512], psB[7][:, :]))
                    wt("t", cs, sbx)
                wt("y", cv, cv.n)
                ldbv = y_dma(stream("qrow3"), slab[0:1, 1024:1024 + D], vrs("bv", 0, D))
                wt("v", stream("qrow3"), ldbv)
                wt("v", cs, cs.n)
                VN(lambda v: v.tensor_tensor(slab[0:1, 1024:1024 + D], slab[0:1, 1024:1024 + D],
                                             slab[0:1, 0:D], OP.mult))
                VD()
                cvbvp = VI(lambda v: v.tensor_copy(bvp_bf[:], slab[0:1, 1024:1024 + D]))
                wt("t", cv, cvbvp)
            return cs.n

        sbc = {}

        def emit_io_chunk(w, c):
            sl = c % 2
            wi0 = WS.index(w)
            if c >= 2:
                wt("y", cv, quant_v.get((w, "io", c - 2), 0))
            elif wi0 > 0:
                wt("y", cv, quant_v.get((WS[wi0 - 1], "io", NCH - 2 + c), 0))
            else:
                wt("y", cv, cv.n)
            li = y_dma(qio[sl], sqrot[1][:, sl * 1024:(sl + 1) * 1024],
                       WT[w].rearrange("(c p) n -> p c n", p=128)[:, c, :])
            if c == 0:
                wt("v", cs, sbc[w])
            wt("v", qio[sl], li)
            VN(lambda v, w=w, sl=sl: v.tensor_tensor(
                sqrot[1][:, sl * 1024:(sl + 1) * 1024],
                sqrot[1][:, sl * 1024:(sl + 1) * 1024], rb[w], OP.mult))
            VD()
            VN(lambda v, sl=sl: v.tensor_copy(iq_t[:],
                                              sqrot[1][:, sl * 1024:(sl + 1) * 1024]))
            VD()
            quant_v[(w, "io", c)] = VI(lambda v, w=w, c=c: v.tensor_scalar(
                wqi[w][:, c, :], iq_t[:], -127.0, 127.0, OP.max, OP.min))
            VD()

        q_p2 = sbt("q_p2", [128, T], BF16)
        k_p2 = sbt("k_p2", [128, T], BF16)
        v_p2 = sbt("v_p2", [128, NTT, 128], BF16)
        qsq2 = sbt("qsq2", [128, T], BF16)
        pt2 = sbt("pt2", [128, 4, 2048], BF16)
        recipb_s = sbt("recipb_s", [128, 512], F32)
        aev = {}
        sc2_ev = {}
        pt2_mm = {}
        pend = [None]

        def emit_proj_mms(p, w, g, bank, first_gate=0):
            if first_gate:
                wt("t", cv, first_gate)
            wt("t", cv, aev.get(bank, 0))
            for c in range(NCH):
                emitter = TN if c < NCH - 1 else TI
                emitter(lambda t, w=w, c=c, g=g, p=p, bank=bank: t.matmul(
                    psB[bank][:, :], wqi[w][:, c, p * 128:(p + 1) * 128],
                    xn[:, c, g * 512:(g + 1) * 512],
                    start=(c == 0), stop=(c == NCH - 1)))
            return ct.n

        def emit_proj_evac(p, w, g, bank, tmm=None):
            dst = q_p2 if w == "q" else k_p2
            bnm = "bq" if w == "q" else "bk"
            if tmm:
                wt("v", ct, tmm)
            aev[bank] = VI(lambda v, w=w, dst=dst, g=g, p=p, bank=bank, bnm=bnm:
                           v.tensor_scalar(dst[:, g * 512:(g + 1) * 512],
                                           psB[bank][:, :], scC[w][:, p:p + 1],
                                           vcs(bnm, p), OP.mult, OP.add))

        def emit_proj_qk(p, w, g):
            bank = g % 2
            tmm = emit_proj_mms(p, w, g, bank)
            emit_proj_evac(p, w, g, bank, tmm)

        issue_am("q", 0, 0)
        issue_am("q", 1, 0)
        for c in range(NCH):
            emit_reduce("q", 0, c)
        sbc["q"] = emit_dance("q")
        tq_last = 0
        for wi in (1, 2):
            wp, wn = WS[wi - 1], WS[wi]
            for c in range(NCH):
                emit_io_chunk(wp, c)
                emit_reduce(wn, wi, c)
                if wi == 2 and c % 2 == 1:
                    tq_last = emit_proj_mms(
                        0, "q", c // 2, c // 2,
                        first_gate=quant_v[("q", "io", NCH - 1)] if c == 1 else 0)
            if wi == 2:
                for g in range(NTG):
                    emit_proj_evac(0, "q", g, g, tq_last if g == 0 else None)
            sbc[wn] = emit_dance(wn)
        tk_last = 0
        for c in range(NCH):
            emit_io_chunk("v", c)
            if c % 2 == 1:
                tk_last = emit_proj_mms(
                    0, "k", c // 2, c // 2,
                    first_gate=quant_v[("k", "io", NCH - 1)] if c == 1 else 0)
        for g in range(NTG):
            emit_proj_evac(0, "k", g, g, tk_last if g == 0 else None)
        wt("t", cv, cv.n)

        def emit_ctx(pgrp, psl, par):
            b0, b1 = (6, 7) if par == 0 else (7, 6)
            for h in range(2):
                for tkk in range(2):
                    tk = 2 * pgrp + tkk
                    off = (2 * h + tkk) * 512
                    last = (h == 1 and tkk == 1)
                    emitter = TI if last else TN
                    if h == 0:
                        emitter(lambda t, tk=tk, psl=psl, off=off, b=b0: t.matmul(
                            psB[b][0:64, :], v_p2[:, tk, 0:64],
                            pt2[:, psl, off:off + 512],
                            start=(tk == 0), stop=(tk == NTT - 1),
                            tile_position=(0, 0), skip_group_check=True))
                    else:
                        emitter(lambda t, tk=tk, psl=psl, off=off, b=b1: t.matmul(
                            psB[b][64:128, :], v_p2[:, tk, 64:128],
                            pt2[:, psl, off:off + 512],
                            start=(tk == 0), stop=(tk == NTT - 1),
                            tile_position=(0, 64), skip_group_check=True))

        def emit_norm():
            pp, gg, par, rc = pend[0]
            pend[0] = None
            b0, b1 = (6, 7) if par == 0 else (7, 6)
            wt("t", cv, rc)
            TI(lambda t: t.matmul(psB[4][:, :], onesr_f[:], sqrot[1][0:1, 0:512],
                                  start=True, stop=True))
            t_b1 = TI(lambda t: t.matmul(psB[5][:, :], onesr_f[:],
                                         sqrot[1][0:1, 512:1024], start=True, stop=True))
            wt("s", ct, t_b1)
            SN(lambda s: s.copy(recipb_s[0:64, :], psB[4][0:64, :]))
            s_rb = SI(lambda s: s.copy(recipb_s[64:128, :], psB[5][64:128, :]))
            sc2_ev["rb"] = s_rb
            wt("v", cs, s_rb)
            VN(lambda v, pp=pp, gg=gg, b=b0: v.tensor_tensor(
                smix[0:64, pp, gg * 512:(gg + 1) * 512], psB[b][0:64, :],
                recipb_s[0:64, :], OP.mult))
            aev[("ctx", par)] = VI(lambda v, pp=pp, gg=gg, b=b1: v.tensor_tensor(
                smix[64:128, pp, gg * 512:(gg + 1) * 512], psB[b][64:128, :],
                recipb_s[64:128, :], OP.mult))

        for p in range(8):
            if p > 0:
                wt("v", ct, ct.n)
                wt("t", cs, cs.n)
            if p > 0:
                for w in ("q", "k"):
                    for g in range(NTG):
                        emit_proj_qk(p, w, g)
            for tt in range(NTT):
                bank = 2 + tt % 2
                wt("t", cv, aev.get(bank, 0))
                for c in range(NCH):
                    TN(lambda t, c=c, tt=tt, p=p, bank=bank: t.matmul(
                        psB[bank][:, 0:128], xn[:, c, tt * 128:(tt + 1) * 128],
                        wqi["v"][:, c, p * 128:(p + 1) * 128],
                        start=(c == 0), stop=False))
                tvm = TI(lambda t, tt=tt, p=p, bank=bank: t.matmul(
                    psB[bank][:, 0:128], onesr_bf[:], bvp_bf[0:1, p * 128:(p + 1) * 128],
                    start=False, stop=True))
                wt("v", ct, tvm)
                aev[bank] = VI(lambda v, tt=tt, p=p, bank=bank: v.tensor_tensor(
                    v_p2[:, tt, :], psB[bank][:, 0:128],
                    sc_b[:, p * 128:(p + 1) * 128], OP.mult))
            proj_done = cv.n
            wt("v", ct, ct.n)
            cvq2a = VI(lambda v, p=p: v.tensor_tensor(qsq2[:], q_p2[:], q_p2[:], OP.mult))
            wt("t", cv, cvq2a)
            for g in range(NTG):
                emitter = TN if g < NTG - 1 else TI
                emitter(lambda t, g=g: t.matmul(
                    psB[g % 2][0:2, 0:512], ones2h_bf[:],
                    qsq2[:, g * 512:(g + 1) * 512], start=True, stop=True))
            tqn = ct.n
            wt("v", ct, tqn)
            for g in range(NTG):
                VN(lambda v, g=g: v.reduce_max(col["part"][0:2, 8 + g:9 + g],
                                               psB[g % 2][0:2, 0:512], axis=AX.X))
            VD()
            cvq2b = VI(lambda v, p=p: v.tensor_tensor(qsq2[:], k_p2[:], k_p2[:], OP.mult))
            wt("t", cv, cvq2b)
            for g in range(NTG):
                emitter = TN if g < NTG - 1 else TI
                emitter(lambda t, g=g: t.matmul(
                    psB[2 + g % 2][0:2, 0:512], ones2h_bf[:],
                    qsq2[:, g * 512:(g + 1) * 512], start=True, stop=True))
            tkn = ct.n
            wt("v", ct, tkn)
            for g in range(NTG):
                VN(lambda v, g=g: v.reduce_max(col["part"][0:2, 12 + g:13 + g],
                                               psB[2 + g % 2][0:2, 0:512], axis=AX.X))
            VD()
            VN(lambda v: v.reduce_max(col["part"][0:2, 4:5], col["part"][0:2, 8:12], axis=AX.X))
            cvmx = VI(lambda v: v.reduce_max(col["part"][0:2, 5:6], col["part"][0:2, 12:16], axis=AX.X))
            VD()
            wt("s", cv, cvmx)
            SN(lambda s: s.activation(col["part"][0:2, 4:5], col["part"][0:2, 4:5], AF.Sqrt))
            s_sq = SI(lambda s: s.activation(col["part"][0:2, 5:6], col["part"][0:2, 5:6], AF.Sqrt))
            wt("v", cs, s_sq)
            VN(lambda v: v.tensor_tensor(col["part"][0:2, 6:7], col["part"][0:2, 4:5],
                                         col["part"][0:2, 5:6], OP.mult))
            VD()
            cvnb = VI(lambda v: v.tensor_scalar(col["part"][0:2, 6:7], col["part"][0:2, 6:7],
                                                -0.125, None, OP.mult))
            wt("t", cv, cvnb)
            t_tb = TI(lambda t: t.transpose(psB[4][0:1, 0:2], col["part"][0:2, 6:7],
                                            eye[0:2, 0:2]))
            wt("s", ct, t_tb)
            s_tb = SI(lambda s: s.copy(tiny[0:1, 16:18], psB[4][0:1, 0:2]))
            wt("t", cs, s_tb)
            for h in range(2):
                tb = TI(lambda t, h=h: t.matmul(psB[4 + h][:, 0:1], onesr_f[:],
                                                tiny[0:1, 16 + h:17 + h], start=True, stop=True))
            wt("s", ct, ct.n)
            SN(lambda s: s.copy(col["tmpa"][:, 4:5], psB[4][:, 0:1]))
            s_nb = SI(lambda s: s.copy(col["tmpa"][:, 5:6], psB[5][:, 0:1]))
            wt("t", cs, s_nb)
            for g in range(NTG):
                par = (p * NTG + g) % 2
                for grp in range(8):
                    ptsl = grp % 4
                    t_half = {}
                    for h in range(2):
                        wt("t", cs, sc2_ev.get("A" if h == 0 else "B", 0))
                        for tkk in range(2):
                            tk = 2 * grp + tkk
                            bank = 2 * h + tkk
                            emitter = TI if tkk == 1 else TN
                            emitter(lambda t, tk=tk, g=g, h=h, bank=bank: t.matmul(
                                psB[bank][:, :], k_p2[64 * h:64 * h + 64, tk * 128:(tk + 1) * 128],
                                q_p2[64 * h:64 * h + 64, g * 512:(g + 1) * 512],
                                start=True, stop=True, tile_position=(64 * h, 0)))
                        t_half[h] = ct.n
                    t_sc = ct.n
                    if grp >= 2:
                        pgrp = grp - 2
                        psl = pgrp % 4
                        if grp == 2:
                            if pend[0] is not None:
                                emit_norm()
                            wt("t", cs, sc2_ev.get("rb", 0))
                            wt("t", cv, aev.get(("ctx", par), 0))
                        for h in range(2):
                            for tkk in range(2):
                                tk = 2 * pgrp + tkk
                                off = (2 * h + tkk) * 512
                                TN(lambda t, tk=tk, psl=psl, off=off, h=h: t.matmul(
                                    psB[4 + h][0:1, 0:512], onesc_bf[:],
                                    pt2[:, psl, off:off + 512],
                                    start=(tk == 0), stop=(tk == NTT - 1)))
                        emit_ctx(pgrp, psl, par)
                        pt2_mm[psl] = ct.n
                    wt("s", ct, t_half[0])
                    wt("s", ct, pt2_mm.get(ptsl, 0))
                    sc2_ev["A"] = SI(lambda s, ptsl=ptsl: s.activation(
                        pt2[:, ptsl, 0:1024], psALL[:, 0:2, :], AF.Exp,
                        bias=col["tmpa"][:, 4:5], scale=0.125))
                    wt("s", ct, t_sc)
                    sc2_ev["B"] = SI(lambda s, ptsl=ptsl: s.activation(
                        pt2[:, ptsl, 1024:2048], psALL[:, 2:4, :], AF.Exp,
                        bias=col["tmpa"][:, 5:6], scale=0.125))
                wt("t", cs, sc2_ev["B"])
                for pgrp in (6, 7):
                    psl = pgrp % 4
                    for h in range(2):
                        for tkk in range(2):
                            tk = 2 * pgrp + tkk
                            off = (2 * h + tkk) * 512
                            TN(lambda t, tk=tk, psl=psl, off=off, h=h: t.matmul(
                                psB[4 + h][0:1, 0:512], onesc_bf[:],
                                pt2[:, psl, off:off + 512],
                                start=(tk == 0), stop=(tk == NTT - 1)))
                for pgrp in (6, 7):
                    emit_ctx(pgrp, pgrp % 4, par)
                t_ctx = ct.n
                pt2_mm[2] = pt2_mm[3] = t_ctx
                wt("s", ct, t_ctx)
                SN(lambda s: s.copy(sqrot[1][0:1, 0:512], psB[4][0:1, 0:512]))
                s_rs = SI(lambda s: s.copy(sqrot[1][0:1, 512:1024], psB[5][0:1, 0:512]))
                wt("v", cs, s_rs)
                cv_rc = VI(lambda v: v.reciprocal(sqrot[1][0:1, 0:1024], sqrot[1][0:1, 0:1024]))
                pend[0] = (p, g, par, cv_rc)
        if pend[0] is not None:
            emit_norm()
        dbg_add("ctx1", smix[:, :, :], (128, NCH * T), BF16)

    if phase_limit >= 9:
        sfree("recipb_s", "pt2", "qsq2", "v_p2", "k_p2", "q_p2", "iq_t",
              "wqi_v", "wqi_k", "wqi_q", "bvp_bf", "scC_v", "scC_k", "scC_q",
              "sqrot1", "sqrot0", "bcB", "bcA", "slab2", "slab")
        QT2 = 512
        NQ2 = T // QT2
        hA = sbt("hA", [128, 32, QT2], BF16)
        hB = sbt("hB", [128, 32, QT2], BF16)
        NW = 4
        wsl12 = [sbt(f"w12b{i}", [128, 2, NCH, 128], BF16) for i in range(NW)]
        wslH = [sbt(f"wHb{i}", [128, 32, 128], BF16) for i in range(NW)]
        t1f = [sbt(f"t1f{i}", [128, QT2], F32) for i in range(2)]
        gt2 = [sbt(f"gt2{i}", [128, QT2], BF16) for i in range(2)]
        sg2 = [sbt(f"sg2{i}", [128, QT2], F32) for i in range(2)]
        xc2 = sbt("xc2", [128, NCH, QT2], F32)
        q12 = [stream(f"q12_{i}") for i in range(NW)]
        qH = [stream(f"qH{i}") for i in range(NW)]
        qx2 = stream("qx2")
        qout = [stream("qout0"), stream("qout1")]
        wdma = lambda i: g_dma

        yv = yTo.rearrange("(c p) t -> p c t", p=128)
        wt("g", ct, ct.n)
        wt("g", cv, cv.n)
        wt("y", ct, ct.n)
        wt("y", cv, cv.n)
        h_ev = {}
        w_use = {}
        out_use = {}
        for Qi in range(NQ2):
            Qs = slice(Qi * QT2, (Qi + 1) * QT2)
            wt("y", cv, out_use.get(0, 0))
            wt("y", cv, out_use.get(1, 0))
            for oc in range(NCH):
                y_dma(qx2, xc2[:, oc, :], xspv[:, oc, Qs])
            xq_tot = qx2.n
            for ht in range(32):
                slh = ht % NW
                wt("g", ct, w_use.get(("12", slh), 0))
                wdma(slh)(q12[slh], wsl12[slh][:, 0, :, :], w1wt[ht, :, :])
                lw = wdma(slh)(q12[slh], wsl12[slh][:, 1, :, :], w2wt[ht, :, :])
                wt("t", q12[slh], lw)
                bankA = ht % 2
                bankB = 2 + ht % 2
                wt("t", cs, h_ev.get(("t1", ht % 2), 0))
                wt("t", cv, h_ev.get(("h", ht % 2), 0))
                for c in range(NCH):
                    TN(lambda t, c=c, slh=slh, bankA=bankA, Qs=Qs: t.matmul(
                        psB[bankA][:, :], wsl12[slh][:, 0, c, :], xn[:, c, Qs],
                        start=(c == 0), stop=(c == NCH - 1)))
                for c in range(NCH):
                    emitter = TN if c < NCH - 1 else TI
                    emitter(lambda t, c=c, slh=slh, bankB=bankB, Qs=Qs: t.matmul(
                        psB[bankB][:, :], wsl12[slh][:, 1, c, :], xn[:, c, Qs],
                        start=(c == 0), stop=(c == NCH - 1)))
                tmm = ct.n
                w_use[("12", slh)] = tmm
                wt("s", ct, tmm)
                h_ev[("t1", ht % 2)] = SI(lambda s, ht=ht, bankA=bankA: s.activation(
                    t1f[ht % 2][:], psB[bankA][:, :], AF.Silu, scale=vcs("w1s", ht)))
                wt("v", cs, h_ev[("t1", ht % 2)])
                h_ev[("h", ht % 2)] = VI(lambda v, ht=ht, bankB=bankB: v.scalar_tensor_tensor(
                    hA[:, ht, :], psB[bankB][:, :], vcs("w2s", ht), t1f[ht % 2][:],
                    OP.mult, OP.mult))
            wt("t", cv, cv.n)
            for ot in range(32):
                slh = ot % NW
                wt("g", ct, w_use.get(("H", slh), 0))
                lw = wdma(slh)(qH[slh], wslH[slh][:], h1wt[ot, :, :])
                wt("t", qH[slh], lw)
                bank = 4 + ot % 2
                wt("t", cs, h_ev.get(("g1", ot % 2), 0))
                for it in range(32):
                    emitter = TN if it < 31 else TI
                    emitter(lambda t, it=it, slh=slh, bank=bank: t.matmul(
                        psB[bank][:, :], wslH[slh][:, it, :], hA[:, it, :],
                        start=(it == 0), stop=(it == 31)))
                tmm = ct.n
                w_use[("H", slh)] = tmm
                wt("s", ct, tmm)
                h_ev[("g1", ot % 2)] = SI(lambda s, ot=ot, bank=bank: s.activation(
                    hB[:, ot, :], psB[bank][:, :], AF.Gelu, scale=vcs("h1s", ot)))
            wt("t", cs, cs.n)
            for ot in range(32):
                slh = ot % NW
                wt("g", ct, w_use.get(("H", slh), 0))
                lw = wdma(slh)(qH[slh], wslH[slh][:], h2wt[ot, :, :])
                wt("t", qH[slh], lw)
                bank = 4 + ot % 2
                wt("t", cs, h_ev.get(("g2", ot % 2), 0))
                for it in range(32):
                    emitter = TN if it < 31 else TI
                    emitter(lambda t, it=it, slh=slh, bank=bank: t.matmul(
                        psB[bank][:, :], wslH[slh][:, it, :], hB[:, it, :],
                        start=(it == 0), stop=(it == 31)))
                tmm = ct.n
                w_use[("H", slh)] = tmm
                wt("s", ct, tmm)
                h_ev[("g2", ot % 2)] = SI(lambda s, ot=ot, bank=bank: s.activation(
                    hA[:, ot, :], psB[bank][:, :], AF.Gelu, scale=vcs("h2s", ot)))
            wt("t", cs, cs.n)
            for oc in range(NCH):
                slh = oc % NW
                wt("g", ct, w_use.get(("H", slh), 0))
                lw = wdma(slh)(stream(f"q3_{slh}"), wslH[slh][:], w3wt[oc, :, :])
                wt("t", stream(f"q3_{slh}"), lw)
                bank = 6 + oc % 2
                wt("t", cs, h_ev.get(("g3", oc % 2), 0))
                for it in range(32):
                    emitter = TN if it < 31 else TI
                    emitter(lambda t, it=it, slh=slh, bank=bank: t.matmul(
                        psB[bank][:, :], wslH[slh][:, it, :], hA[:, it, :],
                        start=(it == 0), stop=(it == 31)))
                tmm = ct.n
                w_use[("H", slh)] = tmm
                wt("s", ct, tmm)
                wt("s", cv, out_use.get(oc % 2, 0))
                sgt = SI(lambda s, oc=oc, slc=oc % 2, bank=bank: s.activation(
                    gt2[slc][:], psB[bank][:, :], AF.Silu, scale=vcs("w3s", oc)))
                h_ev[("g3", oc % 2)] = sgt
                slc = oc % 2
                wt("v", cs, sgt)
                wt("v", qout[slc], w_use.get(("st", slc), 0))
                VN(lambda v, oc=oc, slc=slc, Qs=Qs: v.tensor_tensor(
                    sg2[slc][:], gt2[slc][:], smix[:, oc, Qs], OP.mult))
                VD()
                wt("v", qx2, xq_tot)
                cvy = VI(lambda v, oc=oc, slc=slc: v.tensor_tensor(
                    sg2[slc][:], sg2[slc][:], xc2[:, oc, :], OP.add))
                out_use[slc] = cvy
                wt("y", cv, cvy)
                qout[slc].n += 16
                Q["y"].append(lambda e, o=yv[:, oc, Qs], i=sg2[slc][:], h=qout[slc].h:
                              e.dma_start(out=o, in_=i).then_inc(h, 16))
                w_use[("st", slc)] = qout[slc].n
        wt("y", qout[0], qout[0].n)
        wt("y", qout[1], qout[1].n)

    wt("y", ct, ct.n)
    wt("y", cs, cs.n)
    wt("y", cv, cv.n)
    wt("y", qst, qst.n)
    for st in dbg_list:
        wt("y", st, st.n)

    blk = es.enter_context(nc.Block())
    def runner(key):
        def body(eng):
            for f in Q[key]:
                f(eng)
        return body
    blk.gpsimd(runner("g"))
    blk.tensor(runner("t"))
    blk.scalar(runner("s"))
    blk.vector(runner("v"))
    blk.sync(runner("y"))
    es.close()
    return nc


def _cc(a):
    return np.ascontiguousarray(a, dtype=np.float32)


def _prep(I):
    o = {}
    o["W1"] = _cc(I["L0_attn_W1"])
    o["W2"] = _cc(I["L0_attn_W2"])
    o["w1lT"] = _cc(I["L0_ffn_w1_l"].T)
    o["w2lT"] = _cc(I["L0_ffn_w2_l"].T)
    o["w1hT"] = _cc(I["L0_ffn_w1_h"].T)
    o["w2hT"] = _cc(I["L0_ffn_w2_h"].T)
    o["WqT"] = _cc(I["L1_attn_Wq"].T)
    o["WkT"] = _cc(I["L1_attn_Wk"].T)
    o["WvT"] = _cc(I["L1_attn_Wv"].T)
    o["Wqo"] = _cc(I["L1_attn_Wq"])
    o["Wko"] = _cc(I["L1_attn_Wk"])
    o["Wvo"] = _cc(I["L1_attn_Wv"])
    def tile_io(wT, nin, nout):
        a = np.asarray(wT, np.float32).reshape(nin // 128, 128, nout // 128, 128)
        return _cc(a.transpose(2, 1, 0, 3).reshape(nout // 128, 128, nin))
    o["w1wt"] = tile_io(I["L1_ffn_w1_w"].T, D, 4096)
    o["w2wt"] = tile_io(I["L1_ffn_w2_w"].T, D, 4096)
    o["h1wt"] = tile_io(I["L1_ffn_h1_w"].T, 4096, 4096)
    o["h2wt"] = tile_io(I["L1_ffn_h2_w"].T, 4096, 4096)
    o["w3wt"] = tile_io(I["L1_ffn_w3_w"].T, 4096, D)
    o["w1ht"] = tile_io(I["L0_ffn_w1_h"].T, D, 4096)
    o["w1lt"] = tile_io(I["L0_ffn_w1_l"].T, D, 512)
    o["w2lt"] = tile_io(I["L0_ffn_w2_l"].T, 512, D)
    W1r = np.asarray(I["L0_attn_W1"], np.float32).reshape(NCH, 128, 32, 128)
    o["W1p"] = _cc(np.stack([
        W1r[:, :, [p, 8 + p, 16 + p, 24 + p], :]
        .transpose(1, 0, 2, 3).reshape(128, NCH, 512)
        for p in range(8)]))

    vcol = np.zeros((128, 288), np.float32)
    off = 0
    def put(v, k):
        nonlocal off
        v = np.asarray(v, np.float32)
        assert v.size == 128 * k, (v.size, k)
        vcol[:, off:off + k] = v.reshape(k, 128).T
        off += k
    put(I["L0_attn_b1"], 32)
    put(I["L0_attn_b2"], 8)
    put(I["L0_attn_gamma"], 8)
    put(I["L0_attn_beta"], 8)
    put(I["L0_ffn_routing_token"], 8)
    put(I["L0_ffn_gamma_l"], 8)
    put(I["L0_ffn_gamma_h"], 8)
    put(I["L0_ffn_b1_l"], 4)
    put(I["L0_ffn_b2_l"], 8)
    put(I["L0_ffn_b1_h"], 32)
    put(I["L1_attn_bq"], 8)
    put(I["L1_attn_bk"], 8)
    put(I["L1_attn_bv"], 8)
    put(I["L1_ffn_w1_s"], 32)
    put(I["L1_ffn_w2_s"], 32)
    put(I["L1_ffn_h1_s"], 32)
    put(I["L1_ffn_h2_s"], 32)
    put(I["L1_ffn_w3_s"], 8)
    o["VCOL"] = vcol

    vrow = np.zeros((1, 8192), np.float32)
    off = 0
    def putr(v, k):
        nonlocal off
        vrow[0, off:off + k] = np.asarray(v, np.float32).ravel()
        off += k
    putr(I["L0_attn_b1"], 4096)
    putr(I["L0_ffn_b2_h"], 1024)
    putr(I["L1_attn_bv"], 1024)
    putr(I["L0_attn_b2"], 1024)
    o["VROW"] = vrow

    cst = np.zeros((128, 260), np.float32)
    cst[:, 0:128] = np.eye(128, dtype=np.float32)
    ii = np.arange(128)
    cst[:, 128:256] = (ii[:, None] <= ii[None, :]).astype(np.float32)
    cst[0:64, 256] = 1.0
    cst[64:128, 257] = 1.0
    cst[:, 258] = 1.0
    o["CONST"] = cst
    return o


def kernel(**inputs):
    shared = _prep(inputs)
    x = np.asarray(inputs["L0_attn_x"], np.float32)
    nc = build_program()
    in_maps = []
    for c in range(8):
        m = dict(shared)
        m["xT"] = _cc(x[c].T)
        in_maps.append(m)
    res = run_bass_kernel_spmd(nc, in_maps, core_ids=list(range(8)))
    out = np.stack([np.asarray(r["yT"]).T for r in res.results])
    return out.astype(np.float32)
